# version 03
# baseline (speedup 1.0000x reference)
import numpy as np

import concourse.bass as bass
import concourse.mybir as mybir
from concourse.bass_utils import run_bass_kernel_spmd

F32 = mybir.dt.float32
BF16 = mybir.dt.bfloat16
AL = mybir.AluOpType
AF = mybir.ActivationFunctionType
AX = mybir.AxisListType

MAGIC = float(np.float32(12582912.0))
F32EPS = float(np.finfo(np.float32).eps)
DKSCALE = float(np.float32(128.0 ** -0.5))
ATTSCALE = float(np.float32(1.0) / np.float32(np.sqrt(np.float32(128.0))))
FWHT_C = float(np.float32(1.0 / np.sqrt(8192.0)))

B_, SEQ, D = 2, 4096, 2048
NH, DK = 16, 128
TLOC, NT = 1024, 8
GU_O, FF_H = 16384, 8192
NC_, GRP = 8, 4

ENGINES = ("sync", "scalar", "vector", "tensor", "gpsimd")

CONST_VALS = [1e-5, 1.0 / 128.0, MAGIC, 127.0, -128.0, 1.0, -1.0, -2.0,
              1.0 / 64.0, DKSCALE, 1e-8, 1.0 / 8192.0, F32EPS, 128.0,
              1.0 / 16777216.0, ATTSCALE, FWHT_C, 0.0, 0.5, 2.0]

STOP_AFTER = "all"


class Prog:
    def __init__(self, nc):
        self.nc = nc
        self.ops = {e: [] for e in ENGINES}
        self.sems = {}
        self.cnt = {}

    def emit(self, eng, fname, *args, waits=(), inc=None, inc_n=1, **kw):
        tok = None
        if inc is not None:
            self.cnt[inc] += inc_n
            tok = (inc, self.cnt[inc])
        wl = tuple((s, (self.cnt[s] if s.startswith("d") else v))
                   for (s, v) in (w for w in waits if w is not None) if v > 0)
        sems = self.sems

        def op(e, _f=fname, _a=args, _k=kw, _w=wl, _i=inc, _n=inc_n):
            for s, v in _w:
                e.wait_ge(sems[s], v)
            r = getattr(e, _f)(*_a, **_k)
            if _i is not None:
                r.then_inc(sems[_i], _n)

        self.ops[eng].append(op)
        return tok

    def wait(self, eng, waits):
        wl = tuple((s, (self.cnt[s] if s.startswith("d") else v))
                   for (s, v) in (w for w in waits if w is not None) if v > 0)
        if not wl:
            return
        sems = self.sems

        def op(e, _w=wl):
            for s, v in _w:
                e.wait_ge(sems[s], v)

        self.ops[eng].append(op)

    def dma(self, q, out, in_, waits=(), sem=None):
        assert sem is not None
        return self.emit(q, "dma_start", waits=waits, inc=sem, inc_n=16,
                         out=out, in_=in_)

    def mm(self, out, lhsT, rhs, start, stop, waits=(), inc=False):
        return self.emit("tensor", "matmul", out, lhsT, rhs, waits=waits,
                         inc=("tt" if inc else None), start=start, stop=stop,
                         skip_group_check=True)

    def tr(self, out, in_, ident, waits=(), inc=False):
        return self.emit("tensor", "transpose", out, in_, ident, waits=waits,
                         inc=("tt" if inc else None))

    def v(self, fname, *args, waits=(), inc=False, **kw):
        return self.emit("vector", fname, *args, waits=waits,
                         inc=("vt" if inc else None), **kw)

    def s(self, fname, *args, waits=(), inc=False, **kw):
        return self.emit("scalar", fname, *args, waits=waits,
                         inc=("st" if inc else None), **kw)

    def g(self, fname, *args, waits=(), inc=False, **kw):
        return self.emit("gpsimd", fname, *args, waits=waits,
                         inc=("gt" if inc else None), **kw)



def build_program(stop_after=STOP_AFTER, debug=False):
    from contextlib import ExitStack
    nc = bass.Bass("TRN2", target_bir_lowering=False, debug=debug)
    p = Prog(nc)
    dt = {}

    def din(name, shape, dtype):
        dt[name] = nc.dram_tensor(name, list(shape), dtype,
                                  kind="ExternalInput").ap()

    def dint(name, shape, dtype):
        dt[name] = nc.dram_tensor(name, list(shape), dtype).ap()

    din("x", (TLOC, D), F32)
    din("att0_s", (1024, D), F32)
    din("fq1t", (128, 128), F32)
    din("fq2t", (128, 128), F32)
    din("fk1t", (128, 128), F32)
    din("fk2t", (128, 128), F32)
    din("fbias", (128, 4), F32)
    din("normw", (128, 1), F32)
    din("gu_s", (256, GU_O), F32)
    din("proj_s", (1024, D), F32)
    din("att1_s", (1024, D), F32)
    din("l1bias_pp", (128, 64), F32)
    din("borow", (1, D), F32)
    din("wg_s", (256, FF_H), F32)
    din("wu_s", (256, FF_H), F32)
    din("wd_s", (1024, D), F32)
    din("cos_t", (TLOC, D), BF16)
    din("sin_t", (TLOC, D), BF16)
    din("ident", (128, 128), BF16)
    din("identf", (128, 128), F32)
    din("h128c", (128, 128), BF16)
    din("b2c", (2, 128), BF16)
    din("b2tc", (128, 2), BF16)
    din("ones64", (64, 1), BF16)
    din("ones128", (128, 1), BF16)
    din("onesrow_f32", (1, 128), F32)
    din("tri64", (64, 64), BF16)
    din("diagmask", (128, 512), BF16)
    din("partflag", (128, 3), F32)
    din("cvals", (128, len(CONST_VALS)), F32)
    din("cvalsb", (128, len(CONST_VALS)), BF16)
    out_ap = nc.dram_tensor("out", [TLOC, D], F32, kind="ExternalOutput").ap()
    dbg = {}
    if stop_after == "dbg0":
        dbg["qm"] = nc.dram_tensor("dbg_qm", [128, 16 * 1024], BF16,
                                   kind="ExternalOutput").ap()
        dbg["km"] = nc.dram_tensor("dbg_km", [128, D], BF16,
                                   kind="ExternalOutput").ap()
        dbg["vc"] = nc.dram_tensor("dbg_vc", [128, D], BF16,
                                   kind="ExternalOutput").ap()
        dbg["kv"] = nc.dram_tensor("dbg_kv", [NH * 128, 128], BF16,
                                   kind="ExternalOutput").ap()
        dbg["ofm"] = nc.dram_tensor("dbg_ofm", [128, D], BF16,
                                    kind="ExternalOutput").ap()
        dbg["pre"] = nc.dram_tensor("dbg_pre", [TLOC, D], F32,
                                    kind="ExternalOutput").ap()

    dint("agi_att0", (128, 128, 128), BF16)
    dint("ago_att0", (1024, 128, 128), BF16)
    dint("agi_gu", (64, 128, 512), BF16)
    dint("ago_gu", (512, 128, 512), BF16)
    dint("agi_proj", (32, 128, 512), BF16)
    dint("ago_proj", (256, 128, 512), BF16)
    dint("agi_att1", (128, 128, 128), BF16)
    dint("ago_att1", (1024, 128, 128), BF16)
    dint("agi_wg", (32, 128, 512), BF16)
    dint("ago_wg", (256, 128, 512), BF16)
    dint("agi_wu", (32, 128, 512), BF16)
    dint("ago_wu", (256, 128, 512), BF16)
    dint("agi_wd", (32, 128, 512), BF16)
    dint("ago_wd", (256, 128, 512), BF16)
    dint("agi_sums", (1, 4), F32)
    dint("ago_sums", (1, 4), F32)
    dint("agi_kv0", (NT * NH * 128, 128), BF16)
    dint("ago_kv0", (4 * NT * NH * 128, 128), BF16)
    dint("agi_kv1", (256, 128, 128), BF16)
    dint("ago_kv1", (1024, 128, 128), BF16)
    dint("km_d", (NT, 128, D), BF16)
    dint("vc_d", (NT, 128, D), BF16)
    dint("ktm_d", (NT, 128, D), BF16)
    dint("qtm_d", (NT, 128, D), BF16)
    dint("ofm_d", (NT * 16, 128, 128), BF16)

    es = ExitStack()
    sb = {}

    def sbuf(name, shape, dtype):
        sb[name] = es.enter_context(nc.sbuf_tensor(name, list(shape), dtype))

    def psum(name, shape=(128, 512)):
        sb[name] = es.enter_context(nc.psum_tensor(name, list(shape), F32))

    sbuf("strm", (128, NT * D), F32)
    sbuf("bufA", (128, 16 * 1024), BF16)
    sbuf("bufH", (128, 64 * 512), BF16)
    sbuf("xfmh", (128, 16 * 256), BF16)
    sbuf("scS", (128, 6144), BF16)
    sbuf("fsc", (128, 512), F32)
    sbuf("bk", (128, 2048), BF16)
    sbuf("bv", (128, 2048), BF16)
    sbuf("bfs", (128, 2048), BF16)
    sbuf("wb0", (128, 512), BF16)
    sbuf("wb1", (128, 512), BF16)
    sbuf("ab0", (128, 128), BF16)
    sbuf("ab1", (128, 128), BF16)
    sbuf("sml", (128, 320), F32)
    sbuf("smlb", (128, 256), BF16)
    sbuf("cstb", (128, 840), BF16)
    sbuf("l1b", (128, 64), F32)
    sbuf("cstf", (128, 400), F32)
    sbuf("c2r", (2, 128), BF16)
    sbuf("r1", (1, 384), F32)
    sbuf("cvf", (128, len(CONST_VALS)), F32)
    sbuf("cvb", (128, len(CONST_VALS)), BF16)
    for i in range(6):
        psum("pp%d" % i)
    sb["ppb0"] = es.enter_context(nc.psum_tensor("ppb0", [128, 512], BF16))
    sb["ppb1"] = es.enter_context(nc.psum_tensor("ppb1", [128, 512], BF16))
    for i, cval in enumerate(CONST_VALS):
        nc.const_aps.register(F32, cval, sb["cvf"][:, i:i + 1])
        nc.const_aps.register(BF16, cval, sb["cvb"][:, i:i + 1])

    for s_ in ("dx", "dcs", "dfq", "dag", "dkm", "dbk", "dbv", "dsc",
               "dwb0", "dwb1", "dab0", "dab1", "dwo", "dof", "ddbg", "dout",
               "dk1", "dk2", "dk3", "dk4",
               "tt", "vt", "st", "gt", "cc"):
        p.sems[s_] = es.enter_context(nc.semaphore(s_))
        p.cnt[s_] = 0

    strm, bufA, bufH = sb["strm"], sb["bufA"], sb["bufH"]
    xfmh, scS = sb["xfmh"], sb["scS"]
    bk, bv, bfs = sb["bk"], sb["bv"], sb["bfs"]
    wb = [sb["wb0"], sb["wb1"]]
    ab = [sb["ab0"], sb["ab1"]]
    sml, smlb = sb["sml"], sb["smlb"]
    cstb, cstf, c2r, r1 = sb["cstb"], sb["cstf"], sb["c2r"], sb["r1"]
    pp = [sb["pp%d" % i] for i in range(6)]
    ppb = [sb["ppb0"], sb["ppb1"]]

    fsc = sb["fsc"]

    IDENT = cstb[:, 0:128]
    H128T = cstb[:, 128:256]
    TRI64 = cstb[0:64, 256:320]
    DIAG = cstb[:, 320:832]
    ONES64 = cstb[0:64, 832:833]
    ONES128 = cstb[:, 833:834]
    IDENTF = cstf[:, 0:128]
    PFLAG = lambda r: cstf[:, 128 + r:129 + r]
    FB = lambda i: cstf[:, 131 + i:132 + i]
    NORMW = cstf[:, 135:136]
    FK1BC = cstf[:, 136:264]
    FK2BC = cstf[:, 264:392]
    FB1S = cstf[:, 392:393]
    ONESROW = r1[0:1, 256:384]

    tx = [p.dma("sync", strm[:, j * D:(j + 1) * D],
                 dt["x"][j * 128:(j + 1) * 128, :], sem="dx")
          for j in range(NT)]
    p.dma("gpsimd", sb["cvf"][:, :], dt["cvals"][:, :], sem="dcs")
    p.dma("gpsimd", sb["cvb"][:, :], dt["cvalsb"][:, :], sem="dcs")
    p.dma("gpsimd", cstb[:, 0:128], dt["ident"][:, :], sem="dcs")
    p.dma("gpsimd", cstb[:, 128:256], dt["h128c"][:, :], sem="dcs")
    p.dma("gpsimd", cstb[0:64, 256:320], dt["tri64"][:, :], sem="dcs")
    p.dma("gpsimd", cstb[:, 320:832], dt["diagmask"][:, :], sem="dcs")
    p.dma("gpsimd", cstb[0:64, 832:833], dt["ones64"][:, :], sem="dcs")
    p.dma("gpsimd", cstb[:, 833:834], dt["ones128"][:, :], sem="dcs")
    p.dma("gpsimd", cstf[:, 0:128], dt["identf"][:, :], sem="dcs")
    p.dma("gpsimd", cstf[:, 128:131], dt["partflag"][:, :], sem="dcs")
    p.dma("gpsimd", sb["l1b"][:, :], dt["l1bias_pp"][:, :], sem="dcs")
    p.dma("gpsimd", cstf[:, 131:135], dt["fbias"][:, :], sem="dcs")
    p.dma("gpsimd", cstf[:, 135:136], dt["normw"][:, :], sem="dcs")
    p.dma("gpsimd", r1[0:1, 256:384], dt["onesrow_f32"][:, :], sem="dcs")
    p.dma("gpsimd", c2r[:, 0:128], dt["b2c"][:, :], sem="dcs")
    tconst = p.dma("gpsimd", cstb[:, 834:836], dt["b2tc"][:, :], sem="dcs")

    for _e in ("vector", "scalar", "tensor", "gpsimd"):
        p.wait(_e, [tconst])

    FQ1, FQ2 = bufH[:, 0:128], bufH[:, 128:256]
    FK1, FK2 = bufH[:, 256:384], bufH[:, 384:512]
    p.dma("sync", fsc[:, 0:128], dt["fq1t"][:, :], sem="dfq")
    p.dma("sync", fsc[:, 128:256], dt["fq2t"][:, :], sem="dfq")
    p.dma("sync", fsc[:, 256:384], dt["fk1t"][:, :], sem="dfq")
    tq = p.dma("sync", fsc[:, 384:512], dt["fk2t"][:, :], sem="dfq")
    p.v("tensor_scalar", FQ1, fsc[:, 0:128], DKSCALE, None, op0=AL.mult,
        waits=[tq])
    p.v("tensor_copy", FQ2, fsc[:, 128:256], waits=[tq])
    p.v("tensor_copy", FK1, fsc[:, 256:384], waits=[tq])
    tfq = p.v("tensor_copy", FK2, fsc[:, 384:512], waits=[tq], inc=True)
    p.v("tensor_scalar", FB1S, FB(0), DKSCALE, None, op0=AL.mult,
        waits=[tconst])
    p.wait("tensor", [tconst])
    t1 = p.tr(pp[0][0:1, 0:128], FB(2), IDENTF, inc=True)
    t2 = p.tr(pp[1][0:1, 0:128], FB(3), IDENTF, inc=True)
    tr1 = p.v("tensor_copy", r1[0:1, 0:128], pp[0][0:1, 0:128], waits=[t1],
              inc=True)
    tr2 = p.v("tensor_copy", r1[0:1, 128:256], pp[1][0:1, 0:128], waits=[t2],
              inc=True)
    tb1 = p.mm(pp[2][:, 0:128], ONESROW, r1[0:1, 0:128], True, True,
               waits=[tr1], inc=True)
    tb2 = p.mm(pp[3][:, 0:128], ONESROW, r1[0:1, 128:256], True, True,
               waits=[tr2], inc=True)
    tk1 = p.v("tensor_copy", FK1BC, pp[2][:, 0:128], waits=[tb1], inc=True)
    tk2 = p.v("tensor_copy", FK2BC, pp[3][:, 0:128], waits=[tb2], inc=True)

    bank_free = {i: None for i in range(6)}
    bank_free["b0"] = None
    bank_free["b1"] = None
    bank_free[0] = tr1
    bank_free[1] = tr2
    bank_free[2] = tk1
    bank_free[3] = tk2

    def bw(i):
        t = bank_free[i]
        return [t] if t else []

    t_fsc_free = None
    ag_last = None
    for m in range(4):
        for kc in range(2):
            tcast = None
            for q4 in range(4):
                tl = p.dma("scalar", fsc[:, :],
                           dt["att0_s"][(m * 2 + kc) * 128:
                                        (m * 2 + kc + 1) * 128,
                                        q4 * 512:(q4 + 1) * 512],
                           waits=([t_fsc_free] if t_fsc_free else []) + [tfq],
                           sem="dk1")
                wprev = [ag_last] if (ag_last and q4 == 0) else []
                if m == 3:
                    tcast = p.v("tensor_scalar",
                                bfs[:, q4 * 512:(q4 + 1) * 512], fsc[:, :],
                                NORMW, None, op0=AL.mult,
                                waits=[tl, tconst] + wprev, inc=True)
                else:
                    tcast = p.v("tensor_copy",
                                bfs[:, q4 * 512:(q4 + 1) * 512], fsc[:, :],
                                waits=[tl] + wprev, inc=True)
                t_fsc_free = tcast
            blk0 = (m * 16) * 2 + kc
            ag_last = p.dma("gpsimd",
                            dt["agi_att0"][blk0:blk0 + 31:2, :, :]
                            .rearrange("b p f -> p b f"),
                            bfs[:, :].rearrange("p (b f) -> p b f", f=128),
                            waits=[tcast], sem="dag")
    p.wait("gpsimd", [ag_last])
    cc_att0 = p.emit("gpsimd", "collective_compute", "AllGather", AL.bypass,
                     replica_groups=[list(range(8))],
                     ins=[dt["agi_att0"].opt()], outs=[dt["ago_att0"].opt()],
                     inc="cc")
    t_piece_free = {0: ag_last, 1: ag_last}

    def att0_blk(m, kch, oc):
        return (kch // 2) * 128 + (m * 16 + oc) * 2 + (kch % 2)

    tA_q = {}
    t_km = {}
    t_vc = {}
    t_sml_free = ("vt", 0)
    t_xfm_texture = None
    wb_free = [None, None]
    t_bfs_free = ag_last
    t_fsc2_free = None
    p.wait("tensor", [cc_att0, tfq, tx[-1]])
    for qtr in range(4):
        txf_last = None
        for j2 in range(2):
            j = qtr * 2 + j2
            tcastx = p.s("copy", bfs[:, :], strm[:, j * D:(j + 1) * D],
                         waits=[tx[j], t_piece_free[0], t_piece_free[1]]
                         + ([t_bfs_free] if t_bfs_free else [])
                         + ([t_xfm_texture] if t_xfm_texture else []),
                         inc=True)
            for ch in range(16):
                bi = "b%d" % (ch % 2)
                pb_ = ppb[ch % 2]
                ttr = p.tr(pb_[:, 0:128], bfs[:, ch * 128:(ch + 1) * 128],
                           IDENT, waits=[tcastx] + bw(bi), inc=True)
                tev = p.v("tensor_copy",
                          xfmh[:, ch * 256 + j2 * 128:ch * 256 + j2 * 128 + 128],
                          pb_[:, 0:128], waits=[ttr], inc=True)
                bank_free[bi] = tev
                txf_last = tev
            t_bfs_free = ttr
        t_xfm_ready = txf_last

        for m in range(3):
            for oc in range(NH):
                pb = 4 + (oc % 2)
                tmm = None
                for kch in range(16):
                    wi = kch % 2
                    twl = p.dma("sync", wb[wi][:, 0:128],
                                dt["ago_att0"][att0_blk(m, kch, oc), :, :],
                                waits=([wb_free[wi]] if wb_free[wi] else [])
                                + ([cc_att0] if (qtr == 0 and m == 0 and oc == 0)
                                   else []),
                                sem="dwb%d" % wi)
                    tmm = p.mm(pp[pb][:, 0:256], wb[wi][:, 0:128],
                               xfmh[:, kch * 256:(kch + 1) * 256],
                               start=(kch == 0), stop=(kch == 15),
                               waits=[twl, t_xfm_ready] + (bw(pb) if kch == 0 else []),
                               inc=True)
                    wb_free[wi] = tmm
                h = oc
                te = p.s("copy", bfs[:, 0:256], pp[pb][:, 0:256],
                         waits=[tmm] + ([t_bfs_free] if t_bfs_free else []),
                         inc=True)
                bank_free[pb] = te
                if m == 0:
                    tm1 = p.mm(pp[2][:, 0:256], FQ1, bfs[:, 0:256],
                               True, True, waits=[te] + bw(2), inc=True)
                    tm2 = p.mm(pp[3][:, 0:256], FQ2, bfs[:, 0:256],
                               True, True, waits=[te] + bw(3), inc=True)
                    te2 = p.v("tensor_scalar", bfs[:, 512:768],
                              pp[3][:, 0:256], FB(1), None, op0=AL.add,
                              waits=[tm2, t_piece_free[0], t_piece_free[1]],
                              inc=True)
                    t_piece_free = {0: te2, 1: te2}
                    tqm = p.v("scalar_tensor_tensor",
                              bufA[:, h * 1024 + qtr * 256:
                                   h * 1024 + qtr * 256 + 256],
                              pp[2][:, 0:256], FB1S, bfs[:, 512:768],
                              op0=AL.add, op1=AL.mult,
                              waits=[tm1, te2, tk1], inc=True)
                    tA_q[(h, qtr)] = tqm
                    bank_free[2] = tqm
                    bank_free[3] = te2
                    t_bfs_free = tqm
                elif m == 1:
                    t_last = None
                    for sub in range(2):
                        j = qtr * 2 + sub
                        a_ap = bfs[:, sub * 128:(sub + 1) * 128]
                        tm1 = p.mm(pp[2][:, 0:128], a_ap, FK1,
                                   True, True, waits=[te] + bw(2), inc=True)
                        tm2 = p.mm(pp[3][:, 0:128], a_ap, FK2,
                                   True, True, waits=[te] + bw(3), inc=True)
                        tv1 = p.v("tensor_tensor", sml[:, 64:192],
                                  pp[2][:, 0:128], FK1BC, op=AL.add,
                                  waits=[tm1, tk1, t_sml_free], inc=True)
                        tv2 = p.v("tensor_tensor", sml[:, 192:320],
                                  pp[3][:, 0:128], FK2BC, op=AL.add,
                                  waits=[tm2, tk2, t_sml_free], inc=True)
                        tkm = p.v("tensor_tensor", bfs[:, 512 + sub * 128:
                                                       512 + (sub + 1) * 128],
                                  sml[:, 64:192], sml[:, 192:320], op=AL.mult,
                                  waits=[tv1, tv2, t_piece_free[sub]],
                                  inc=True)
                        t_sml_free = tkm
                        bank_free[2] = tv1
                        bank_free[3] = tv2
                        t_km[(j, h)] = p.dma(
                            "gpsimd", dt["km_d"][j, :, h * 128:(h + 1) * 128],
                            bfs[:, 512 + sub * 128:512 + (sub + 1) * 128],
                            waits=[tkm], sem="dkm")
                        t_piece_free[sub] = t_km[(j, h)]
                        t_last = tkm
                    t_bfs_free = t_last
                else:
                    t_last = None
                    for sub in range(2):
                        j = qtr * 2 + sub
                        bi = "b%d" % (sub % 2)
                        ttr = p.tr(ppb[sub % 2][:, 0:128],
                                   bfs[:, sub * 128:(sub + 1) * 128],
                                   IDENT, waits=[te] + bw(bi), inc=True)
                        tev = p.v("tensor_copy",
                                  bfs[:, 512 + sub * 128:512 + (sub + 1) * 128],
                                  ppb[sub % 2][:, 0:128],
                                  waits=[ttr, t_piece_free[sub]], inc=True)
                        bank_free[bi] = tev
                        t_vc[(j, h)] = p.dma(
                            "gpsimd", dt["vc_d"][j, :, h * 128:(h + 1) * 128],
                            bfs[:, 512 + sub * 128:512 + (sub + 1) * 128],
                            waits=[tev], sem="dkm")
                        t_piece_free[sub] = t_vc[(j, h)]
                        t_last = ttr
                    t_bfs_free = t_last
        t_xfm_texture = tmm

    t_bk_free, t_bv_free = None, None
    t_fsc_dma = None
    tkv_last = None
    for j in range(NT):
        tlk = p.dma("sync", bk[:, :], dt["km_d"][j, :, :],
                    waits=[t_km[(j, NH - 1)]]
                    + ([t_bk_free] if t_bk_free else []), sem="dbk")
        tlv = p.dma("sync", bv[:, :], dt["vc_d"][j, :, :],
                    waits=[t_vc[(j, NH - 1)]]
                    + ([t_bv_free] if t_bv_free else []), sem="dbv")
        tev = None
        for h in range(NH):
            bi = h % 2
            tmm = p.mm(pp[bi][:, 0:128], bk[:, h * 128:(h + 1) * 128],
                       bv[:, h * 128:(h + 1) * 128], True, True,
                       waits=[tlk, tlv] + bw(bi), inc=True)
            tev = p.v("tensor_copy", bfs[:, h * 128:(h + 1) * 128],
                      pp[bi][:, 0:128],
                      waits=[tmm]
                      + ([t_fsc_dma] if (h == 0 and t_fsc_dma) else [])
                      + ([t_piece_free[0], t_piece_free[1], t_bfs_free]
                         if (h == 0 and t_bfs_free) else []),
                      inc=True)
            bank_free[bi] = tev
            t_bk_free = tmm
            t_bv_free = tmm
        t_bfs_free = None
        t_fsc_dma = p.dma("gpsimd",
                          dt["agi_kv0"][j * NH * 128:(j + 1) * NH * 128, :]
                          .rearrange("(h p) f -> p h f", p=128),
                          bfs[:, :].rearrange("p (h f) -> p h f", f=128),
                          waits=[tev], sem="dag")
        tkv_last = t_fsc_dma
    p.wait("gpsimd", [tkv_last])
    cc_kv0 = p.emit("gpsimd", "collective_compute", "AllGather", AL.bypass,
                    replica_groups=[[0, 1, 2, 3], [4, 5, 6, 7]],
                    ins=[dt["agi_kv0"].opt()], outs=[dt["ago_kv0"].opt()],
                    inc="cc")

    SFULL = scS[:, 0:2048]
    SWORK = scS[:, 2048:4096]
    SCIN = scS[:, 4096:6144]
    SWH = lambda h: scS[:, 2048 + h * 128:2048 + (h + 1) * 128]

    def kv0_ap(rank, slot):
        r0 = (rank * NT * NH + slot * NH) * 128
        return (dt["ago_kv0"][r0:r0 + NH * 128, :]
                .rearrange("(h p) f -> p h f", p=128))

    t_sf = p.v("memset", SFULL, 0.0, waits=[t_fsc_dma], inc=True)
    t_rms_sd = None
    t_rms_in = None
    t_scin_free = None
    t_swork_tread = None
    t_ofm_w = []
    t_ofm_ev_last = None
    for j in range(NT):
        if j > 0:
            for rk in range(4):
                tld = p.dma("scalar",
                            SCIN.rearrange("p (h f) -> p h f", f=128),
                            kv0_ap(rk, j - 1),
                            waits=[cc_kv0]
                            + ([t_scin_free] if t_scin_free else []),
                            sem="dsc")
                t_sf = p.v("tensor_tensor", SFULL, SFULL, SCIN, op=AL.add,
                           waits=[tld, t_sf], inc=True)
                t_scin_free = t_sf
        t_sw = p.v("tensor_copy", SWORK, SFULL, waits=[t_sf]
                   + ([t_swork_tread] if t_swork_tread else []), inc=True)
        for rk in range(3):
            tld = p.dma("scalar",
                        SCIN.rearrange("p (h f) -> p h f", f=128),
                        kv0_ap(rk, j),
                        waits=[cc_kv0] + ([t_scin_free] if t_scin_free else []),
                        sem="dsc")
            t_sw = p.v("scalar_tensor_tensor", SWORK, SCIN, PFLAG(rk), SWORK,
                       op0=AL.mult, op1=AL.add, waits=[tld, t_sw], inc=True)
            t_scin_free = t_sw
        tlk = p.dma("sync", bk[:, :], dt["km_d"][j, :, :],
                    waits=[t_bk_free] if t_bk_free else [], sem="dbk")
        tlv = p.dma("sync", bv[:, :], dt["vc_d"][j, :, :],
                    waits=[t_bv_free] if t_bv_free else [], sem="dbv")
        t_kf_tread = None
        for h in range(NH):
            ttr = p.tr(ppb[0][:, 0:128], bk[:, h * 128:(h + 1) * 128], IDENT,
                       waits=[tlk] + bw("b0"), inc=True)
            tkf = p.v("tensor_copy", bfs[:, 0:128], ppb[0][:, 0:128],
                      waits=[ttr, t_fsc_dma]
                      + ([t_kf_tread] if t_kf_tread else []),
                      inc=True)
            bank_free["b0"] = tkf
            ti2s = []
            t_after_kv = None
            for c2 in range(2):
                pb0 = c2 * 64
                qm_ap = bufA[:, h * 1024 + j * 128 + pb0:
                             h * 1024 + j * 128 + pb0 + 64]
                tst_ = p.mm(pp[1][pb0:pb0 + 64, 0:64], bfs[:, pb0:pb0 + 64],
                            qm_ap, True, True,
                            waits=[tkf, tA_q[(h, j // 2)]]
                            + (bw(1) if c2 == 0 else []),
                            inc=True)
                tmask = p.v("tensor_tensor", smlb[pb0:pb0 + 64, 0:64],
                            pp[1][pb0:pb0 + 64, 0:64], TRI64, op=AL.mult,
                            waits=[tst_, tconst], inc=True)
                bank_free[1] = tmask
                ti1 = p.mm(pp[2][pb0:pb0 + 64, 0:128],
                           smlb[pb0:pb0 + 64, 0:64],
                           bv[pb0:pb0 + 64, h * 128:(h + 1) * 128],
                           start=True, stop=False,
                           waits=[tmask, tlv] + (bw(2) if c2 == 0 else []))
                w_in = [t_sw] if c2 == 0 else [t_after_kv]
                ti2 = p.mm(pp[2][pb0:pb0 + 64, 0:128], qm_ap, SWH(h),
                           start=False, stop=True, waits=w_in, inc=True)
                ti2s.append(ti2)
                if c2 == 0:
                    tkv = p.mm(pp[3][:, 0:128],
                               bk[0:64, h * 128:(h + 1) * 128],
                               bv[0:64, h * 128:(h + 1) * 128], True, True,
                               waits=[tlk, tlv] + bw(3), inc=True)
                    t_after_kv = p.v("tensor_tensor", SWH(h), SWH(h),
                                     pp[3][:, 0:128], op=AL.add,
                                     waits=[tkv, ti2, t_sw], inc=True)
                    bank_free[3] = t_after_kv
            t_swork_tread = ti2s[-1]
            t_kf_tread = tst_
            tsq = p.s("activation", bfs[:, 128:256], pp[2][:, 0:128],
                      AF.Square, accum_out=sml[:, 0:1],
                      waits=ti2s + [t_fsc_dma], inc=True)
            tva = p.v("tensor_scalar", sml[:, 1:2], sml[:, 0:1],
                      1.0 / 128.0, 1e-5, op0=AL.mult, op1=AL.add,
                      waits=[tsq, t_rms_sd], inc=True)
            tsd = p.s("activation", sml[:, 2:3], sml[:, 1:2], AF.Sqrt,
                      waits=[tva, t_rms_in], inc=True)
            t_rms_sd = tsd
            tin = p.v("reciprocal", sml[:, 3:4], sml[:, 2:3], waits=[tsd],
                      inc=True)
            t_rms_in = tin
            ton = p.v("tensor_scalar", smlb[:, 128:256], pp[2][:, 0:128],
                      sml[:, 3:4], None, op0=AL.mult,
                      waits=[tin] + bw("b1"), inc=True)
            bank_free[2] = ton
            ttr2 = p.tr(ppb[1][:, 0:128], smlb[:, 128:256], IDENT,
                        waits=[ton] + bw("b1"), inc=True)
            tev = p.v("tensor_copy", xfmh[:, h * 128:(h + 1) * 128],
                      ppb[1][:, 0:128],
                      waits=[ttr2]
                      + ([t_ofm_w[-1]] if (h == 0 and t_ofm_w) else []),
                      inc=True)
            bank_free["b1"] = tev
            t_ofm_ev_last = tev
        t_bk_free = tst_
        t_bv_free = ti2s[-1]
        t_ofm_w.append(p.dma("gpsimd",
                             dt["ofm_d"][j * 16:(j + 1) * 16, :, :]
                             .rearrange("b p f -> p b f"),
                             xfmh[:, 0:2048].rearrange("p (b f) -> p b f",
                                                       f=128),
                             waits=[t_ofm_ev_last], sem="dof"))

    ab_free = [None, None]
    res_tok = {}
    for og in range(4):
        for jb in range(2):
            for h in range(NH):
                wi = h % 2
                blk0 = att0_blk(3, h, og * 4)
                twl = p.dma("scalar",
                            wb[wi][:, 0:512].rearrange("p (b f) -> p b f", f=128),
                            dt["ago_att0"][blk0:blk0 + 7:2, :, :]
                            .rearrange("b p f -> p b f"),
                            waits=([wb_free[wi]] if wb_free[wi] else []),
                            sem="dwb%d" % wi)
                tmmlast = None
                for j4 in range(4):
                    j = jb * 4 + j4
                    ai = (h * 4 + j4) % 2
                    tal = p.dma("sync", ab[ai][:, 0:128],
                                dt["ofm_d"][j * 16 + h, :, :],
                                waits=[t_ofm_w[j]]
                                + ([ab_free[ai]] if ab_free[ai] else []),
                                sem="dab%d" % ai)
                    tmm = p.mm(pp[j4][:, 0:512], ab[ai][:, 0:128],
                               wb[wi][:, 0:512],
                               start=(h == 0), stop=(h == NH - 1),
                               waits=[twl, tal] + (bw(j4) if h == 0 else []),
                               inc=True)
                    ab_free[ai] = tmm
                    tmmlast = tmm
                    if h == NH - 1:
                        tres = p.v("tensor_tensor",
                                   strm[:, j * D + og * 512:j * D + og * 512 + 512],
                                   strm[:, j * D + og * 512:j * D + og * 512 + 512],
                                   pp[j4][:, 0:512], op=AL.add, waits=[tmm],
                                   inc=True)
                        res_tok[(og, j)] = tres
                        bank_free[j4] = tres
                wb_free[wi] = tmmlast
    t_block = [res_tok[(3, j)] for j in range(NT)]

    def layernorm_pass(t_ready_per_tile, make_fm):
        toks = []
        t_bfs_l = [None]
        tsd = None
        for j in range(NT):
            xt = strm[:, j * D:(j + 1) * D]
            tb = None
            for q4 in range(4):
                tb = p.v("bn_stats", sml[:, 8 + q4 * 6:14 + q4 * 6],
                         xt[:, q4 * 512:(q4 + 1) * 512],
                         waits=[t_ready_per_tile[j]], inc=True)
            ta = p.v("bn_aggr", sml[:, 32:34], sml[:, 8:32],
                     waits=[tb, tsd], inc=True)
            tsd = p.s("activation", sml[:, 34:35], sml[:, 33:34], AF.Sqrt,
                      bias=1e-5, waits=[ta], inc=True)
            trc = p.v("reciprocal", sml[:, 35:36], sml[:, 34:35], waits=[tsd],
                      inc=True)
            tap = p.v("tensor_scalar", xt, xt, sml[:, 32:33], sml[:, 35:36],
                      op0=AL.subtract, op1=AL.mult, waits=[trc], inc=True)
            if make_fm:
                tcast = p.s("copy", bfs[:, :], xt,
                            waits=[tap] + ([t_bfs_l[0]] if t_bfs_l[0] else []),
                            inc=True)
                for ch in range(16):
                    bi = "b%d" % (ch % 2)
                    ttr = p.tr(ppb[ch % 2][:, 0:128],
                               bfs[:, ch * 128:(ch + 1) * 128], IDENT,
                               waits=[tcast] + bw(bi), inc=True)
                    tev = p.v("tensor_copy",
                              bufA[:, ch * 1024 + j * 128:
                                   ch * 1024 + j * 128 + 128],
                              ppb[ch % 2][:, 0:128], waits=[ttr], inc=True)
                    bank_free[bi] = tev
                t_bfs_l[0] = ttr
                toks.append(tev)
            else:
                toks.append(tap)
        return toks

    if stop_after == "dbg0":
        tdbg = p.dma("gpsimd", dbg["qm"][:, :], bufA[:, :],
                     waits=[tA_q[(15, 3)]], sem="ddbg")
        tdbg = p.dma("gpsimd", dbg["km"][:, :], dt["km_d"][0, :, :],
                     waits=[t_km[(0, 15)]], sem="ddbg")
        tdbg = p.dma("gpsimd", dbg["vc"][:, :], dt["vc_d"][0, :, :],
                     waits=[t_vc[(0, 15)]], sem="ddbg")
        tdbg = p.dma("gpsimd", dbg["kv"][:, :],
                     dt["agi_kv0"][0:NH * 128, :], waits=[tkv_last], sem="ddbg")
        tdbg = p.dma("gpsimd", dbg["ofm"][:, :],
                     dt["ofm_d"][0:16, :, :].rearrange("b p f -> p b f"),
                     waits=[t_ofm_w[0]], sem="ddbg")
        for j in range(NT):
            tdbg = p.dma("gpsimd", dbg["pre"][j * 128:(j + 1) * 128, :],
                         strm[:, j * D:(j + 1) * D],
                         waits=[res_tok[(3, j)]], sem="ddbg")
        p.wait("gpsimd", [tdbg])
        t_block = [tdbg] * NT

    def finish(tok_list):
        tout = None
        for j in range(NT):
            tout = p.dma("gpsimd", out_ap[j * 128:(j + 1) * 128, :],
                         strm[:, j * D:(j + 1) * D],
                         waits=[tok_list[min(j, len(tok_list) - 1)]],
                         sem="dout")
        p.wait("gpsimd", [tout])

    ln1 = layernorm_pass(t_block,
                         make_fm=(stop_after not in ("l0attn", "dbg0")))

    if stop_after in ("l0attn", "dbg0"):
        finish(ln1)
        build_blocks(nc, p)
        es.close()
        return nc

    t_bk_prep = t_bk_free
    t_bv_prep = t_bv_free
    t_fsc_free = None
    ag2_last = None

    def prep_ternary(src_name, nkc, ncols, agi_name, oc_per_kc):
        nonlocal t_bk_prep, t_bv_prep, t_fsc_free, t_ag_guard
        t_loc = {"fsc": t_fsc_free, "bk": None}
        tw_last = t_ag_guard
        nns = ncols // 512
        for kc in range(nkc):
            for ot in range(ncols // 2048):
                tcast = None
                tabs = None
                for q4 in range(4):
                    w_f = []
                    if t_loc["fsc"]:
                        w_f.append(t_loc["fsc"])
                    tl = p.dma("scalar", fsc[:, :],
                               dt[src_name][kc * 128:(kc + 1) * 128,
                                            ot * 2048 + q4 * 512:
                                            ot * 2048 + (q4 + 1) * 512],
                               waits=w_f, sem="dk1")
                    sl = slice(q4 * 512, (q4 + 1) * 512)
                    wb_ = ([tw_last] if (tw_last and q4 == 0) else [])
                    if t_bk_prep is not None and q4 == 0:
                        wb_ = wb_ + [t_bk_prep]
                    tcast = p.s("copy", bfs[:, sl], fsc[:, :],
                                waits=[tl] + wb_, inc=True)
                    tabs = p.s("activation", bk[:, sl], fsc[:, :], AF.Abs,
                               waits=[tl] + ([tw_last] if tw_last else [])
                               + ([t_bk_prep] if t_bk_prep else []),
                               inc=True)
                    t_bk_prep = None
                    t_loc["fsc"] = tabs
                    t_fsc_free = tabs
                tsum = {}
                for ns in range(4):
                    tsum[ns] = p.mm(
                        pp[ns][0:2, 0:512], cstb[:, 834:836],
                        bk[:, ns * 512:(ns + 1) * 512],
                        True, True, waits=[tabs] + bw(ns), inc=True)
                tssc = {}
                for ns in range(4):
                    tssc[ns] = p.v("tensor_scalar", bv[0:2, ns * 512:(ns + 1) * 512],
                                   pp[ns][0:2, 0:512], 1.0 / 64.0, 1e-8,
                                   op0=AL.mult, op1=AL.max,
                                   waits=[tsum[ns]]
                                   + ([t_bv_prep] if (ns == 0 and t_bv_prep) else []),
                                   inc=True)
                    bank_free[ns] = tssc[ns]
                t_bv_prep = None
                for ns in range(4):
                    sl = slice(ns * 512, (ns + 1) * 512)
                    pb_ = 4 + ns % 2
                    tbc = p.mm(pp[pb_][:, 0:512], c2r[:, 0:128],
                               bv[0:2, ns * 512:(ns + 1) * 512], True, True,
                               waits=[tssc[ns]] + bw(pb_), inc=True)
                    trc = p.v("reciprocal", fsc[:, :], pp[pb_][:, 0:512],
                              waits=[tbc, t_loc["fsc"]], inc=True)
                    tu = p.v("tensor_tensor", bk[:, sl], bfs[:, sl],
                             fsc[:, :], op=AL.mult,
                             waits=[trc, tcast, tabs], inc=True)
                    tcl = p.v("tensor_scalar", fsc[:, :], bk[:, sl],
                              1.0, -1.0, op0=AL.min, op1=AL.max,
                              waits=[tu], inc=True)
                    trd = p.v("tensor_scalar", fsc[:, :], fsc[:, :],
                              MAGIC, None, op0=AL.add,
                              waits=[tcl], inc=True)
                    trd = p.v("tensor_scalar", fsc[:, :], fsc[:, :],
                              MAGIC, None, op0=AL.subtract,
                              waits=[trd], inc=True)
                    twh = p.v("tensor_tensor", bk[:, sl], fsc[:, :],
                              pp[pb_][:, 0:512], op=AL.mult,
                              waits=[trd, tu], inc=True)
                    t_loc["fsc"] = twh
                    t_fsc_free = twh
                    bank_free[pb_] = twh
                    td = p.v("tensor_tensor", bk[:, sl], bk[:, sl],
                             bfs[:, sl], op=AL.subtract, waits=[twh],
                             inc=True)
                    twe = p.v("tensor_tensor", bk[:, sl], bk[:, sl],
                              bfs[:, sl], op=AL.add, waits=[td], inc=True)
                    nk_ = 2 if oc_per_kc == 128 else 8
                    blk0 = (ot * 4 + ns) * nk_ + kc
                    tw_last = p.dma("gpsimd",
                                    dt[agi_name][blk0, :, :],
                                    bk[:, sl], waits=[twe], sem="dag")
        t_ag_guard = tw_last
        return tw_last

    t_ag_guard = None
    t_bk_prep = ln1[-1]
    tw_gu = prep_ternary("gu_s", 2, GU_O, "agi_gu", 128)
    p.wait("gpsimd", [tw_gu])
    cc_gu = p.emit("gpsimd", "collective_compute", "AllGather", AL.bypass,
                   replica_groups=[list(range(8))],
                   ins=[dt["agi_gu"].opt()], outs=[dt["ago_gu"].opt()],
                   inc="cc")
    tw_pr = prep_ternary("proj_s", 8, D, "agi_proj", 16)
    p.wait("gpsimd", [tw_pr])
    cc_pr = p.emit("gpsimd", "collective_compute", "AllGather", AL.bypass,
                   replica_groups=[list(range(8))],
                   ins=[dt["agi_proj"].opt()], outs=[dt["ago_proj"].opt()],
                   inc="cc")

    def gu_blk(k, oc):
        return (k // 2) * 256 + (k % 2) * 128 + oc

    def proj_blk(k, oc):
        return (k // 8) * 128 + (k % 8) * 16 + oc

    def ffn_l0(x_fm_toks):
        res2 = {}
        wbf = [None, None]
        t_bk_ffn = t_ag_guard
        t_bv_ffn = t_ag_guard
        t_h_free = tfq
        t_bfs_f = ("vt", p.cnt["vt"])
        t_fsc_f = ("vt", p.cnt["vt"])
        SLOT_SEM = ["dbk", "dbv", "dk2", "dk3"]
        slot_free = [t_bk_ffn, t_bv_ffn, t_bk_ffn, t_bv_ffn]
        for half in range(2):
            tstat = None
            t_invb = None
            hw = [x_fm_toks[-1]] if half == 0 else [x_fm_toks[-1]]
            t_h = {}
            t_up_stage = t_bfs_f
            for g in range(16):
                tupst = {}
                for gu in (1, 0):
                    ocg = g + (16 if gu == 1 else 0)
                    tmm4 = [None] * 4
                    for kch in range(16):
                        sl_ = kch % 4
                        blk = (kch // 2) * 64 + ocg * 2 + (kch % 2)
                        twl = p.dma(["sync", "scalar"][kch % 2],
                                    bk[:, sl_ * 512:(sl_ + 1) * 512],
                                    dt["ago_gu"][blk, :, :],
                                    waits=([slot_free[sl_]] if slot_free[sl_]
                                           else [(SLOT_SEM[sl_], 0)])
                                    + ([cc_gu] if (half == 0 and g == 0
                                                   and gu == 1 and kch < 4)
                                       else []),
                                    sem=SLOT_SEM[sl_])
                        for c in range(4):
                            tmm4[c] = p.mm(
                                pp[c][:, 0:512],
                                bk[:, sl_ * 512 + c * 128:sl_ * 512 + c * 128 + 128],
                                bufA[:, kch * 1024 + half * 512:
                                     kch * 1024 + half * 512 + 512],
                                start=(kch == 0), stop=(kch == 15),
                                waits=[twl] + hw + (bw(c) if kch == 0 else []),
                                inc=(c == 3))
                        slot_free[sl_] = tmm4[3]
                    if gu == 1:
                        for c in range(4):
                            tupst[c] = p.v("tensor_copy",
                                           bfs[:, c * 512:(c + 1) * 512],
                                           pp[c][:, 0:512],
                                           waits=[tmm4[3], t_up_stage],
                                           inc=True)
                            bank_free[c] = tupst[c]
                for c in range(4):
                    i = g * 4 + c
                    tsg = p.s("activation", fsc[:, :], pp[c][:, 0:512],
                              AF.Silu, waits=[tmm4[3]]
                              + ([t_fsc_f] if t_fsc_f else []), inc=True)
                    t_fsc_f = tsg
                    hsl = slice(i * 512, (i + 1) * 512)
                    th = p.v("tensor_tensor", bufH[:, hsl], fsc[:, :],
                             bfs[:, c * 512:(c + 1) * 512], op=AL.mult,
                             waits=[tsg, tupst[c]]
                             + ([t_h_free] if (i == 0 and t_h_free) else []),
                             inc=True)
                    t_h[i] = th
                    bank_free[c] = th
                    th2 = p.s("activation", smlb[:, 0:128] if False else
                              bfs[:, c * 512:(c + 1) * 512], bufH[:, hsl],
                              AF.Square, waits=[th], inc=True)
                    tstat = p.mm(pp[4][0:1, 0:512],
                                 ONES128, bfs[:, c * 512:(c + 1) * 512],
                                 start=(i == 0), stop=(i == 63),
                                 waits=[th2] + ([t_invb] if (i == 0 and t_invb)
                                                else []),
                                 inc=(i == 63))
                t_up_stage = tstat
            t_h_free = None
            t_bk_ffn = slot_free[0]
            t_bv_ffn = slot_free[1]
            t_bfs_f = tstat
            tms = p.v("tensor_scalar", fsc[0:1, 0:512], pp[4][0:1, 0:512],
                      1.0 / 8192.0, F32EPS, op0=AL.mult, op1=AL.add,
                      waits=[tstat, t_fsc_f if t_fsc_f else ("vt", 0)],
                      inc=True)
            bank_free[4] = tms
            tsq2 = p.s("activation", pp[5][0:1, 0:512], fsc[0:1, 0:512],
                       AF.Sqrt, waits=[tms] + bw(5), inc=True)
            tinv = p.v("reciprocal", fsc[0:1, 0:512], pp[5][0:1, 0:512],
                       waits=[tsq2], inc=True)
            t_invb = p.mm(pp[5][:, 0:512], ONESROW, fsc[0:1, 0:512],
                          True, True, waits=[tinv], inc=True)
            t_fsc_f = t_invb
            t_hn = {}
            for i in range(64):
                hsl = slice(i * 512, (i + 1) * 512)
                t_hn[i] = p.v("tensor_tensor", bufH[:, hsl], bufH[:, hsl],
                              pp[5][:, 0:512], op=AL.mult,
                              waits=[t_invb, t_h[i]], inc=True)
            bank_free[5] = t_hn[63]
            for og2 in range(4):
                tmm4 = [None] * 4
                for kch in range(64):
                    sl_ = kch % 4
                    blk = (kch // 8) * 32 + og2 * 8 + (kch % 8)
                    twl = p.dma(["sync", "scalar"][kch % 2],
                                bk[:, sl_ * 512:(sl_ + 1) * 512],
                                dt["ago_proj"][blk, :, :],
                                waits=([slot_free[sl_]] if slot_free[sl_]
                                       else [(SLOT_SEM[sl_], 0)])
                                + ([cc_pr] if (half == 0 and og2 == 0
                                               and kch < 4) else []),
                                sem=SLOT_SEM[sl_])
                    for c in range(4):
                        tmm4[c] = p.mm(
                            pp[c][:, 0:512],
                            bk[:, sl_ * 512 + c * 128:sl_ * 512 + c * 128 + 128],
                            bufH[:, kch * 512:(kch + 1) * 512],
                            start=(kch == 0), stop=(kch == 63),
                            waits=[twl, t_hn[kch]] + (bw(c) if kch == 0 else []),
                            inc=(c == 3))
                    slot_free[sl_] = tmm4[3]
                for c in range(4):
                    ot = og2 * 4 + c
                    tev = p.s("copy", bfs[:, 0:512], pp[c][:, 0:512],
                              waits=[tmm4[3], t_bfs_f], inc=True)
                    bank_free[c] = tev
                    for sub in range(4):
                        j = half * 4 + sub
                        bi = "b%d" % (sub % 2)
                        ttr = p.tr(ppb[sub % 2][:, 0:128],
                                   bfs[:, sub * 128:(sub + 1) * 128], IDENT,
                                   waits=[tev] + bw(bi), inc=True)
                        tres = p.v("tensor_tensor",
                                   strm[:, j * D + ot * 128:j * D + ot * 128 + 128],
                                   strm[:, j * D + ot * 128:j * D + ot * 128 + 128],
                                   ppb[sub % 2][:, 0:128], op=AL.add,
                                   waits=[ttr] + ([res2[(j, ot)]] if (j, ot) in res2
                                                  else []),
                                   inc=True)
                        res2[(j, ot)] = tres
                        bank_free[bi] = tres
                    t_bfs_f = ttr
            t_bk_ffn = slot_free[0]
            t_bv_ffn = slot_free[1]
        return [res2[(j, 15)] for j in range(NT)]

    rtoks = ffn_l0(ln1)
    ln2 = layernorm_pass(rtoks, make_fm=(stop_after != "l0ffn"))

    if stop_after == "l0ffn":
        finish(ln2)
        build_blocks(nc, p)
        es.close()
        return nc

    ag1_last = t_ag_guard
    for m in range(4):
        for kc in range(2):
            tcast = None
            for q4 in range(4):
                tl = p.dma("scalar", fsc[:, :],
                           dt["att1_s"][(m * 2 + kc) * 128:
                                        (m * 2 + kc + 1) * 128,
                                        q4 * 512:(q4 + 1) * 512],
                           waits=[t_fsc_free], sem="dk1")
                tcast = p.v("tensor_copy", bfs[:, q4 * 512:(q4 + 1) * 512],
                            fsc[:, :],
                            waits=[tl] + ([ag1_last] if q4 == 0 else []),
                            inc=True)
                t_fsc_free = tcast
            blk0 = (m * 16) * 2 + kc
            ag1_last = p.dma("gpsimd",
                             dt["agi_att1"][blk0:blk0 + 31:2, :, :]
                             .rearrange("b p f -> p b f"),
                             bfs[:, :].rearrange("p (b f) -> p b f", f=128),
                             waits=[tcast], sem="dag")
    p.wait("gpsimd", [ag1_last])
    cc_att1 = p.emit("gpsimd", "collective_compute", "AllGather", AL.bypass,
                     replica_groups=[list(range(8))],
                     ins=[dt["agi_att1"].opt()], outs=[dt["ago_att1"].opt()],
                     inc="cc")
    t_ag_guard = ag1_last

    tpart = {}
    t_acc_last = None
    t_prev_acc = None
    for mi, (nm, nkc) in enumerate((("wg_s", 2), ("wu_s", 2), ("wd_s", 8))):
        ncols = FF_H if nkc == 2 else D
        tacc = None
        for kc in range(nkc):
            for ot in range(ncols // 2048):
                for q4 in range(4):
                    tl = p.dma("scalar", fsc[:, :],
                               dt[nm][kc * 128:(kc + 1) * 128,
                                      ot * 2048 + q4 * 512:
                                      ot * 2048 + (q4 + 1) * 512],
                               waits=[t_fsc_free], sem="dk1")
                    trd_ = p.v("reduce_sum", sml[:, 36:37], fsc[:, :],
                               axis=AX.X, apply_absolute_value=True,
                               waits=[tl, tacc if tacc else t_prev_acc],
                               inc=True)
                    t_fsc_free = trd_
                    if tacc is None:
                        tacc = p.v("tensor_copy", sml[:, 37 + mi:38 + mi],
                                   sml[:, 36:37], waits=[trd_], inc=True)
                    else:
                        tacc = p.v("tensor_tensor", sml[:, 37 + mi:38 + mi],
                                   sml[:, 37 + mi:38 + mi], sml[:, 36:37],
                                   op=AL.add, waits=[trd_], inc=True)
        tpart[mi] = p.g("tensor_reduce", sml[0:1, 40 + mi:41 + mi],
                        sml[:, 37 + mi:38 + mi], axis=AX.C, op=AL.add,
                        waits=[tacc], inc=True)
        t_acc_last = tpart[mi]
        t_prev_acc = tacc
    tz4 = p.v("memset", sml[0:1, 43:44], 0.0, waits=[t_acc_last], inc=True)
    tsums = p.dma("gpsimd", dt["agi_sums"][0:1, 0:4], sml[0:1, 40:44],
                  waits=[t_acc_last, tz4], sem="dag")
    p.wait("gpsimd", [tsums])
    cc_sum = p.emit("gpsimd", "collective_compute", "AllReduce", AL.add,
                    replica_groups=[list(range(8))],
                    ins=[dt["agi_sums"].opt()], outs=[dt["ago_sums"].opt()],
                    inc="cc")
    tsl = p.dma("gpsimd", sml[0:1, 44:48], dt["ago_sums"][0:1, 0:4],
                waits=[cc_sum], sem="dcs")
    tmean = {}
    tsw = {}
    for mi in range(3):
        n_el = float(1.0 / 16777216.0)
        tmean[mi] = p.v("tensor_scalar", sml[0:1, 48 + mi:49 + mi],
                        sml[0:1, 44 + mi:45 + mi], 1.0 / 16777216.0, 1e-5,
                        op0=AL.mult, op1=AL.max, waits=[tsl], inc=True)
        tsw[mi] = p.v("reciprocal", sml[0:1, 52 + mi:53 + mi],
                      sml[0:1, 48 + mi:49 + mi], waits=[tmean[mi]], inc=True)
    ISW = lambda mi: cstf[:, 393 + mi:394 + mi]
    SWP = lambda mi: cstf[:, 396 + mi:397 + mi]
    tbc_l = []
    for mi in range(3):
        tm1_ = p.mm(pp[4][:, 0:1], ONESROW, sml[0:1, 48 + mi:49 + mi],
                    True, True, waits=[tmean[mi]] + bw(4), inc=True)
        t1_ = p.v("tensor_copy", ISW(mi), pp[4][:, 0:1], waits=[tm1_],
                  inc=True)
        bank_free[4] = t1_
        tm2_ = p.mm(pp[4][:, 0:1], ONESROW, sml[0:1, 52 + mi:53 + mi],
                    True, True, waits=[tsw[mi]] + bw(4), inc=True)
        t2_ = p.v("tensor_copy", SWP(mi), pp[4][:, 0:1], waits=[tm2_],
                  inc=True)
        bank_free[4] = t2_
        tbc_l += [t1_, t2_]
    t_swbc = tbc_l[-1]

    def wg_blk(k, oc):
        return (k // 2) * 128 + (k % 2) * 64 + oc

    def wd_blk(k, oc):
        return (k // 8) * 128 + (k % 8) * 16 + oc

    for mi, (nm, nkc, agn, opk) in enumerate(
            (("wg_s", 2, "agi_wg", 64), ("wu_s", 2, "agi_wu", 64),
             ("wd_s", 8, "agi_wd", 16))):
        smi = mi if mi < 2 else 2
        ncols = FF_H if nkc == 2 else D
        for kc in range(nkc):
            for ot in range(ncols // 2048):
                for q4 in range(4):
                    tl = p.dma("scalar", fsc[:, :],
                               dt[nm][kc * 128:(kc + 1) * 128,
                                      ot * 2048 + q4 * 512:
                                      ot * 2048 + (q4 + 1) * 512],
                               waits=[t_fsc_free], sem="dk1")
                    tv1_ = p.v("tensor_scalar", fsc[:, :], fsc[:, :],
                               SWP(smi), MAGIC, op0=AL.mult, op1=AL.add,
                               waits=[tl, t_swbc], inc=True)
                    tv2_ = p.v("tensor_scalar", fsc[:, :], fsc[:, :],
                               MAGIC, 1.0, op0=AL.subtract, op1=AL.min,
                               waits=[tv1_], inc=True)
                    tv3_ = p.v("tensor_scalar",
                               bfs[:, q4 * 512:(q4 + 1) * 512], fsc[:, :],
                               -1.0, None, op0=AL.max,
                               waits=[tv2_, t_ag_guard if q4 == 0 else None],
                               inc=True)
                    t_fsc_free = tv3_
                for ns4 in range(4):
                    blk0 = (ot * 4 + ns4) * nkc + kc
                    t_ag_guard = p.dma("gpsimd", dt[agn][blk0, :, :],
                                       bfs[:, ns4 * 512:(ns4 + 1) * 512],
                                       waits=[tv3_], sem="dag")
        p.wait("gpsimd", [t_ag_guard])
        p.emit("gpsimd", "collective_compute", "AllGather", AL.bypass,
               replica_groups=[list(range(8))],
               ins=[dt[agn].opt()], outs=[dt["ago_" + agn[4:]].opt()],
               inc="cc")
    cc_wd = ("cc", p.cnt["cc"])

    def att1_blk(m, kch, oc):
        return (kch // 2) * 128 + (m * 16 + oc) * 2 + (kch % 2)

    wbf = [None, None]
    t_bfs_f = t_ag_guard
    tq_piece = {}
    tk_piece = {}
    tv_piece = {}
    p.wait("tensor", [cc_att1])
    x2rd = ln2[-1]
    for m in range(3):
        for oc in range(NH):
            pb_ = 4 + (oc % 2)
            tmmh = {}
            for kch in range(16):
                wi = kch % 2
                twl = p.dma("sync", wb[wi][:, 0:128],
                            dt["ago_att1"][att1_blk(m, kch, oc), :, :],
                            waits=([wbf[wi]] if wbf[wi] else [])
                            + ([cc_att1] if (m == 0 and oc == 0 and kch < 2)
                               else []),
                            sem="dwb%d" % wi)
                for hf in range(2):
                    tmmh[hf] = p.mm(pp[pb_ - 4 + hf * 2][:, 0:512],
                                    wb[wi][:, 0:128],
                                    bufA[:, kch * 1024 + hf * 512:
                                         kch * 1024 + (hf + 1) * 512],
                                    start=(kch == 0), stop=(kch == 15),
                                    waits=[twl, x2rd]
                                    + (bw(pb_ - 4 + hf * 2) if kch == 0 else []),
                                    inc=True)
                wbf[wi] = tmmh[1]
            for hf in range(2):
                bnk = pb_ - 4 + hf * 2
                te = p.v("tensor_scalar", bfs[:, 0:512], pp[bnk][:, 0:512],
                         sb["l1b"][:, m * 16 + oc:m * 16 + oc + 1], None,
                         op0=AL.add, waits=[tmmh[hf], t_bfs_f], inc=True)
                bank_free[bnk] = te
                for sub in range(4):
                    j = hf * 4 + sub
                    bi = "b%d" % (sub % 2)
                    ttr = p.tr(ppb[sub % 2][:, 0:128],
                               bfs[:, sub * 128:(sub + 1) * 128], IDENT,
                               waits=[te] + bw(bi), inc=True)
                    tevp = p.v("tensor_copy",
                               bfs[:, 512 + sub * 128:512 + (sub + 1) * 128],
                               ppb[sub % 2][:, 0:128],
                               waits=[ttr, t_piece_free[sub % 2]], inc=True)
                    bank_free[bi] = tevp
                    if m == 0:
                        tp = p.dma("gpsimd",
                                   dt["qtm_d"][j, :, oc * 128:(oc + 1) * 128],
                                   bfs[:, 512 + sub * 128:512 + (sub + 1) * 128],
                                   waits=[tevp], sem="dkm")
                        tq_piece[(j, oc)] = tp
                    elif m == 1:
                        tp = p.dma("gpsimd",
                                   dt["ktm_d"][j, :, oc * 128:(oc + 1) * 128],
                                   bfs[:, 512 + sub * 128:512 + (sub + 1) * 128],
                                   waits=[tevp], sem="dkm")
                        tk_piece[(j, oc)] = tp
                    else:
                        tp = p.dma("gpsimd",
                                   dt["agi_kv1"][128 + oc * 8 + j, :, :],
                                   bfs[:, 512 + sub * 128:512 + (sub + 1) * 128],
                                   waits=[tevp], sem="dkm")
                        tv_piece[(j, oc)] = tp
                    t_piece_free[sub % 2] = tp
                t_bfs_f = ttr

    COS_ = lambda: bv[:, :]
    SIN_ = lambda: scS[:, 0:2048]
    T1_ = lambda: scS[:, 2048:4096]
    T2_ = lambda: scS[:, 4096:6144]
    t_scs_free = ("vt", p.cnt["vt"])
    t_bk_f2 = ("tt", p.cnt["tt"])
    t_bv_f2 = ("vt", p.cnt["vt"])
    tq_rope = {}
    tk_stage = {}
    for j in range(NT):
        tcl_ = p.dma("sync", bv[:, :], dt["cos_t"][j * 128:(j + 1) * 128, :],
                     waits=[t_bv_f2] + [("dbk", 0), ("dbv", 0), ("dk2", 0), ("dk3", 0)], sem="dbv")
        tsl_ = p.dma("sync", SIN_(), dt["sin_t"][j * 128:(j + 1) * 128, :],
                     waits=[t_scs_free], sem="dsc")
        for qk in range(2):
            dsrc = "qtm_d" if qk == 0 else "ktm_d"
            tld = p.dma("sync", bk[:, :], dt[dsrc][j, :, :],
                        waits=[t_bk_f2,
                               tq_piece[(j, 15)] if qk == 0 else tk_piece[(j, 15)]]
                        + [("dbk", 0), ("dbv", 0), ("dk2", 0), ("dk3", 0)], sem="dbk")
            tt1 = p.v("tensor_tensor", T1_(), bk[:, :], COS_(), op=AL.mult,
                      waits=[tld, tcl_, t_scs_free], inc=True)
            ke = bk[:, :].rearrange("p (a two) -> p a two", two=2)
            se = SIN_().rearrange("p (a two) -> p a two", two=2)
            t2e = T2_().rearrange("p (a two) -> p a two", two=2)
            tt2a = p.v("tensor_tensor", t2e[:, :, 0], ke[:, :, 1],
                       se[:, :, 0], op=AL.mult, waits=[tld, tsl_], inc=True)
            tt2b = p.v("tensor_tensor", t2e[:, :, 1], ke[:, :, 0],
                       se[:, :, 1], op=AL.mult, waits=[tld, tsl_], inc=True)
            t1e = T1_().rearrange("p (a two) -> p a two", two=2)
            oute = bfs[:, :].rearrange("p (a two) -> p a two", two=2)
            to1 = p.v("tensor_tensor", oute[:, :, 0], t1e[:, :, 0],
                      t2e[:, :, 0], op=AL.subtract,
                      waits=[tt1, tt2a, t_bfs_f], inc=True)
            to2 = p.v("tensor_tensor", oute[:, :, 1], t1e[:, :, 1],
                      t2e[:, :, 1], op=AL.add, waits=[tt1, tt2b], inc=True)
            t_scs_free = to2
            t_bk_f2 = to2
            tevl = None
            for ch in range(16):
                bi = "b%d" % (ch % 2)
                ttr = p.tr(ppb[ch % 2][:, 0:128],
                           bfs[:, ch * 128:(ch + 1) * 128], IDENT,
                           waits=[to1, to2] + bw(bi), inc=True)
                if qk == 0:
                    tev = p.v("tensor_copy",
                              bufA[:, ch * 1024 + j * 128:
                                   ch * 1024 + j * 128 + 128],
                              ppb[ch % 2][:, 0:128], waits=[ttr], inc=True)
                    tq_rope[j] = tev
                else:
                    tev = p.v("tensor_copy",
                              bk[:, ch * 128:(ch + 1) * 128],
                              ppb[ch % 2][:, 0:128],
                              waits=[ttr] + ([tk_stage[j]] if (ch == 0 and j in tk_stage) else []),
                              inc=True)
                bank_free[bi] = tev
                tevl = ttr
            t_bfs_f = tevl
            if qk == 1:
                tk_stage[j] = p.dma(
                    "gpsimd",
                    dt["agi_kv1"][j:j + 121:8, :, :]
                    .rearrange("b p f -> p b f"),
                    bk[:, :].rearrange("p (ch t) -> p ch t", t=128),
                    waits=[tev], sem="dkm")
                t_bk_f2 = tk_stage[j]
                t_bv_f2 = tt1
    p.wait("gpsimd", [tk_stage[NT - 1], tv_piece[(NT - 1, NH - 1)]])
    cc_kv1 = p.emit("gpsimd", "collective_compute", "AllGather", AL.bypass,
                    replica_groups=[[0, 1, 2, 3], [4, 5, 6, 7]],
                    ins=[dt["agi_kv1"].opt()], outs=[dt["ago_kv1"].opt()],
                    inc="cc")

    MROW = lambda c: sml[:, 36 + c:37 + c]
    t_ofm_w1 = []
    kbuf_free = [None, None]
    vbuf_free = [None, None]
    t_acc_free = None
    t_ofm_ev = None
    for j in range(NT):
        t_ofm_uses = None
        for h in range(NH):
            nslot = j + 1
            t_l = None
            t_m_prev = None
            t_acc = None
            texp_prev = None
            tcr_prev = None
            for s_ in range(nslot):
                kb = wb[s_ % 2]
                vb_ = [bk, bv][s_ % 2]
                kap = dt["ago_kv1"].rearrange("(r b) p f -> r b p f",
                                              b=256)[:, h * 8 + s_]
                tkl = p.dma("sync",
                            kb[:, 0:512].rearrange("p (r t) -> p r t", t=128),
                            kap.rearrange("r p f -> p r f"),
                            waits=[cc_kv1] + ([kbuf_free[s_ % 2]]
                                              if kbuf_free[s_ % 2] else []),
                            sem="dwb%d" % (s_ % 2))
                hb = h // 8
                vap = dt["ago_kv1"].rearrange("(r b) p f -> r b p f",
                                              b=256)[:, 128 + h * 8 + s_]
                tvl = p.dma("scalar",
                            vb_[:, 0:512].rearrange("p (r t) -> p r t", t=128),
                            vap.rearrange("r p f -> p r f"),
                            waits=[cc_kv1] + ([vbuf_free[s_ % 2]]
                                              if vbuf_free[s_ % 2]
                                              else [("dk2", 0), ("dk3", 0)]),
                            sem="db%s" % ("k" if s_ % 2 == 0 else "v"))
                tsc = p.mm(pp[0][:, 0:512],
                           bufA[:, h * 1024 + j * 128:h * 1024 + j * 128 + 128],
                           kb[:, 0:512], True, True,
                           waits=[tkl, tq_rope[NT - 1]] + bw(0), inc=True)
                kbuf_free[s_ % 2] = tsc
                tmx = p.v("reduce_max", sml[:, 50:51], pp[0][:, 0:512],
                          axis=AX.X, waits=[tsc, t_m_prev], inc=True)
                if s_ == 0:
                    tmn = p.v("tensor_copy", MROW(s_ % 2), sml[:, 50:51],
                              waits=[tmx], inc=True)
                else:
                    tmn = p.v("tensor_tensor", MROW(s_ % 2), sml[:, 50:51],
                              MROW(1 - s_ % 2), op=AL.max, waits=[tmx],
                              inc=True)
                tbx = p.v("tensor_scalar", sml[:, 52:53], MROW(s_ % 2),
                          -ATTSCALE, None, op0=AL.mult,
                          waits=[tmn, texp_prev], inc=True)
                texp = p.s("activation", bfs[:, 0:512], pp[0][:, 0:512],
                           AF.Exp, bias=sml[:, 52:53], scale=ATTSCALE,
                           waits=[tbx, texp_prev], inc=True)
                bank_free[0] = texp
                tmask = None
                if s_ == j:
                    texp = p.v("tensor_tensor", bfs[:, 0:512], bfs[:, 0:512],
                               DIAG, op=AL.mult, waits=[texp], inc=True)
                tps = p.v("reduce_sum", sml[:, 53:54], bfs[:, 0:512],
                          axis=AX.X, waits=[texp], inc=True)
                if s_ == 0:
                    t_l = p.v("tensor_copy", sml[:, 54:55], sml[:, 53:54],
                              waits=[tps], inc=True)
                else:
                    tdm = p.v("tensor_tensor", sml[:, 55:56],
                              MROW(1 - s_ % 2), MROW(s_ % 2), op=AL.subtract,
                              waits=[tmn, tcr_prev], inc=True)
                    tcr = p.s("activation", sml[:, 56:57], sml[:, 55:56],
                              AF.Exp, scale=ATTSCALE,
                              waits=[tdm, t_acc, t_l], inc=True)
                    tcr_prev = tcr
                    t_l = p.v("scalar_tensor_tensor", sml[:, 54:55],
                              sml[:, 54:55], sml[:, 56:57], sml[:, 53:54],
                              op0=AL.mult, op1=AL.add, waits=[tps, tcr, t_l],
                              inc=True)
                tpts = []
                for g in range(4):
                    ttr = p.tr(ppb[g % 2][:, 0:128],
                               bfs[:, g * 128:(g + 1) * 128], IDENT,
                               waits=[texp, tps] + bw("b%d" % (g % 2)),
                               inc=True)
                    tevt = p.v("tensor_copy",
                               bfs[:, 512 + g * 128:512 + (g + 1) * 128],
                               ppb[g % 2][:, 0:128],
                               waits=[ttr, t_piece_free[g % 2]], inc=True)
                    t_piece_free[g % 2] = tevt
                    bank_free["b%d" % (g % 2)] = tevt
                    tpts.append(tevt)
                tpv = None
                for g in range(4):
                    tpv = p.mm(pp[1][:, 0:128],
                               bfs[:, 512 + g * 128:512 + (g + 1) * 128],
                               vb_[:, g * 128:(g + 1) * 128],
                               start=(g == 0), stop=(g == 3),
                               waits=[tpts[g], tvl] + (bw(1) if g == 0 else []),
                               inc=True)
                vbuf_free[s_ % 2] = tpv
                if s_ == 0:
                    t_acc = p.v("tensor_copy", fsc[:, 0:128], pp[1][:, 0:128],
                                waits=[tpv, t_acc_free], inc=True)
                else:
                    t_acc = p.v("scalar_tensor_tensor", fsc[:, 0:128],
                                fsc[:, 0:128], sml[:, 56:57], pp[1][:, 0:128],
                                op0=AL.mult, op1=AL.add,
                                waits=[tpv, t_acc], inc=True)
                bank_free[1] = t_acc
                t_m_prev = tmn
                texp_prev = texp
            trl = p.v("reciprocal", sml[:, 57:58], sml[:, 54:55],
                      waits=[t_l], inc=True)
            to_ = p.v("tensor_scalar", bfs[:, 1024:1152], fsc[:, 0:128],
                      sml[:, 57:58], None, op0=AL.mult,
                      waits=[trl, t_acc], inc=True)
            t_acc_free = to_
            ttro = p.tr(ppb[0][:, 0:128], bfs[:, 1024:1152], IDENT,
                        waits=[to_] + bw("b0"), inc=True)
            t_ofm_uses = p.v("tensor_copy", xfmh[:, h * 128:(h + 1) * 128],
                             ppb[0][:, 0:128],
                             waits=[ttro]
                             + ([t_ofm_w1[-1]] if (h == 0 and t_ofm_w1) else []),
                             inc=True)
            bank_free["b0"] = t_ofm_uses
        t_ofm_w1.append(p.dma("gpsimd",
                              dt["ofm_d"][j * 16:(j + 1) * 16, :, :]
                              .rearrange("b p f -> p b f"),
                              xfmh[:, 0:2048].rearrange("p (b f) -> p b f",
                                                        f=128),
                              waits=[t_ofm_uses], sem="dof"))

    res3 = {}
    ab_free = [None, None]
    t_borow = None
    for og in range(4):
        tbr = p.dma("sync", fsc[0:1, 0:512],
                    dt["borow"][0:1, og * 512:(og + 1) * 512],
                    waits=[t_borow if t_borow else t_acc_free], sem="dk4")
        for jb in range(2):
            for h in range(NH):
                wi = h % 2
                blk0 = att1_blk(3, h, og * 4)
                twl = p.dma("scalar",
                            wb[wi][:, 0:512].rearrange("p (b f) -> p b f", f=128),
                            dt["ago_att1"][blk0:blk0 + 7:2, :, :]
                            .rearrange("b p f -> p b f"),
                            waits=([wbf[wi]] if wbf[wi] else []),
                            sem="dwb%d" % wi)
                for j4 in range(4):
                    j = jb * 4 + j4
                    ai = (h * 4 + j4) % 2
                    tal = p.dma("sync", ab[ai][:, 0:128],
                                dt["ofm_d"][j * 16 + h, :, :],
                                waits=[t_ofm_w1[j]]
                                + ([ab_free[ai]] if ab_free[ai] else []),
                                sem="dab%d" % ai)
                    tmm = p.mm(pp[j4][:, 0:512], ab[ai][:, 0:128],
                               wb[wi][:, 0:512],
                               start=(h == 0), stop=False,
                               waits=[twl, tal] + (bw(j4) if h == 0 else []),
                               inc=True)
                    ab_free[ai] = tmm
                    if h == NH - 1:
                        tmm = p.mm(pp[j4][:, 0:512], ONESROW,
                                   fsc[0:1, 0:512], start=False, stop=True,
                                   waits=[tbr], inc=True)
                        tres = p.v("tensor_tensor",
                                   strm[:, j * D + og * 512:j * D + og * 512 + 512],
                                   strm[:, j * D + og * 512:j * D + og * 512 + 512],
                                   pp[j4][:, 0:512], op=AL.add, waits=[tmm],
                                   inc=True)
                        res3[(og, j)] = tres
                        bank_free[j4] = tres
                wbf[wi] = tmm
        t_borow = tmm
    ln3 = layernorm_pass([res3[(3, j)] for j in range(NT)], make_fm=False)

    if stop_after == "l1attn":
        finish(ln3)
        build_blocks(nc, p)
        es.close()
        return nc

    res4 = {}
    t_bfs_f = ("vt", p.cnt["vt"])
    t_fsc_f = ("vt", p.cnt["vt"])
    t_A_f = ("tt", p.cnt["tt"])
    t_bk_ffn = vbuf_free[0]
    t_bv_ffn = vbuf_free[1]
    SLOT_SEM = ["dbk", "dbv", "dk2", "dk3"]
    slot_free = [t_bk_ffn, t_bv_ffn, t_bk_ffn, t_bv_ffn]
    INVS = lambda tile: sml[:, 60 + tile:61 + tile]
    for half in range(2):
        tq_tok = {}
        for sub in range(4):
            j = half * 4 + sub
            xt = strm[:, j * D:(j + 1) * D]
            tmx = p.v("reduce_max", sml[:, 50:51], xt, axis=AX.X,
                      apply_absolute_value=True, waits=[ln3[j]], inc=True)
            tmc = p.v("tensor_scalar", sml[:, 51:52], sml[:, 50:51], 1e-5,
                      None, op0=AL.max, waits=[tmx], inc=True)
            trc = p.v("reciprocal", sml[:, 52:53], sml[:, 51:52],
                      waits=[tmc], inc=True)
            tst = p.v("tensor_scalar", sml[:, 53:54], sml[:, 52:53], 128.0,
                      None, op0=AL.mult, waits=[trc], inc=True)
            tiv = p.v("tensor_scalar", INVS(j), sml[:, 51:52], 1.0 / 128.0,
                      None, op0=AL.mult, waits=[tmc], inc=True)
            for q4 in range(4):
                sl = slice(q4 * 512, (q4 + 1) * 512)
                tv1_ = p.v("tensor_scalar", fsc[:, :], xt[:, sl],
                           sml[:, 53:54], MAGIC, op0=AL.mult, op1=AL.add,
                           waits=[tst, t_fsc_f], inc=True)
                tv2_ = p.v("tensor_scalar", fsc[:, :], fsc[:, :], MAGIC,
                           127.0, op0=AL.subtract, op1=AL.min,
                           waits=[tv1_], inc=True)
                tv3_ = p.v("tensor_scalar", bfs[:, sl], fsc[:, :], -128.0,
                           None, op0=AL.max, waits=[tv2_, t_bfs_f], inc=True)
                t_fsc_f = tv3_
                t_bfs_f = tv3_
            for ch in range(16):
                bi = "b%d" % (ch % 2)
                ttr = p.tr(ppb[ch % 2][:, 0:128],
                           bfs[:, ch * 128:(ch + 1) * 128], IDENT,
                           waits=[tv3_, t_A_f] + bw(bi), inc=True)
                tev = p.v("tensor_copy",
                          bufA[:, ch * 1024 + half * 512 + sub * 128:
                               ch * 1024 + half * 512 + sub * 128 + 128],
                          ppb[ch % 2][:, 0:128], waits=[ttr], inc=True)
                bank_free[bi] = tev
            tq_tok[sub] = tev
            t_bfs_f = ttr
        if stop_after == "l1f_q" and half == 0:
            finish([tq_tok[3]] * NT)
            build_blocks(nc, p)
            es.close()
            return nc
        trow = []
        for sub in range(4):
            j = half * 4 + sub
            ttr = p.tr(pp[4][0:1, 0:128], INVS(j), IDENTF,
                       waits=[tq_tok[sub]] + bw(4), inc=True)
            tcp = p.v("tensor_copy", fsc[0:1, sub * 128:(sub + 1) * 128],
                      pp[4][0:1, 0:128], waits=[ttr, t_fsc_f], inc=True)
            bank_free[4] = tcp
            trow.append(tcp)
        trg = p.v("tensor_scalar", fsc[1:2, 0:512] if False else fsc[0:1, 0:512],
                  fsc[0:1, 0:512], sml[0:1, 48:49], None, op0=AL.mult,
                  waits=trow, inc=True)
        t_rbg = p.mm(pp[4][:, 0:512], ONESROW, fsc[0:1, 0:512], True, True,
                     waits=[trg], inc=True)
        t_fsc_f = t_rbg
        t_h = {}
        t_up_stage = t_bfs_f
        tg2 = None
        for g in range(16):
            tupst = {}
            tmm4 = [None] * 4
            for gu, agn in ((1, "ago_wu"), (0, "ago_wg")):
                for kch in range(16):
                    sl_ = kch % 4
                    blk = (kch // 2) * 32 + g * 2 + (kch % 2)
                    twl = p.dma(["sync", "scalar"][kch % 2],
                                bk[:, sl_ * 512:(sl_ + 1) * 512],
                                dt[agn][blk, :, :],
                                waits=([slot_free[sl_]] if slot_free[sl_]
                                       else [(SLOT_SEM[sl_], 0)])
                                + ([cc_wd] if (half == 0 and g == 0
                                               and gu == 1 and kch < 4)
                                   else []),
                                sem=SLOT_SEM[sl_])
                    for c in range(4):
                        tmm4[c] = p.mm(
                            pp[c][:, 0:512],
                            bk[:, sl_ * 512 + c * 128:sl_ * 512 + c * 128 + 128],
                            bufA[:, kch * 1024 + half * 512:
                                 kch * 1024 + half * 512 + 512],
                            start=(kch == 0), stop=(kch == 15),
                            waits=[twl, tq_tok[3]] + (bw(c) if kch == 0 else []),
                            inc=(c == 3))
                    slot_free[sl_] = tmm4[3]
                if gu == 1:
                    for c in range(4):
                        tupst[c] = p.v("tensor_copy",
                                       bfs[:, c * 512:(c + 1) * 512],
                                       pp[c][:, 0:512],
                                       waits=[tmm4[3], t_up_stage], inc=True)
                        bank_free[c] = tupst[c]
            tlast_tr = None
            for c in range(4):
                i = g * 4 + c
                tgc = p.v("tensor_copy", fsc[:, :], pp[c][:, 0:512],
                          waits=[tmm4[3], t_fsc_f], inc=True)
                tg2 = p.v("tensor_tensor", fsc[:, :], fsc[:, :],
                          pp[4][:, 0:512], op=AL.mult,
                          waits=[tgc, t_rbg], inc=True)
                tsg = p.s("activation", fsc[:, :], fsc[:, :], AF.Silu,
                          waits=[tg2], inc=True)
                hsl = slice(i * 512, (i + 1) * 512)
                th = p.v("tensor_tensor", bufH[:, hsl], fsc[:, :],
                         bfs[:, c * 512:(c + 1) * 512], op=AL.mult,
                         waits=[tsg, tupst[c]], inc=True)
                t_h[i] = th
                t_fsc_f = th
                bank_free[c] = th
            t_up_stage = th
        bank_free[4] = tg2
        if stop_after == "l1f_gu" and half == 0:
            finish([t_h[63]] * NT)
            build_blocks(nc, p)
            es.close()
            return nc
        t_z = {}
        for m_ in range(32):
            ta = p.mm(pp[0][:, 0:512], H128T, bufH[:, (2 * m_) * 512:
                                                    (2 * m_ + 1) * 512],
                      True, True, waits=[t_h[2 * m_]] + bw(0), inc=True)
            tb = p.mm(pp[1][:, 0:512], H128T, bufH[:, (2 * m_ + 1) * 512:
                                                    (2 * m_ + 2) * 512],
                      True, True, waits=[t_h[2 * m_ + 1]] + bw(1), inc=True)
            tev0 = p.v("tensor_copy", fsc[:, :], pp[0][:, 0:512],
                       waits=[ta, t_fsc_f], inc=True)
            t_fsc_f = tev0
            tza = p.v("tensor_tensor", bufH[:, (2 * m_) * 512:(2 * m_ + 1) * 512],
                      fsc[:, :], pp[1][:, 0:512], op=AL.add,
                      waits=[tev0, tb], inc=True)
            tzb = p.v("tensor_tensor", bufH[:, (2 * m_ + 1) * 512:
                                            (2 * m_ + 2) * 512],
                      fsc[:, :], pp[1][:, 0:512], op=AL.subtract,
                      waits=[tev0, tb], inc=True)
            t_fsc_f = tzb
            t_z[2 * m_] = tza
            t_z[2 * m_ + 1] = tzb
            bank_free[0] = tzb
            bank_free[1] = tzb
        tprev = t_z[63]
        hh = 2
        while hh < 64:
            a_ap = (bufH[:, :].rearrange("p (g w t) -> p g w t",
                                         w=2 * hh, t=512)
                    .rearrange("p g w t -> p (g w t)") if False else None)
            a_ap = bufH[:, :].rearrange("p (g two w t) -> p g two w t",
                                        two=2, w=hh, t=512)
            tsum_ = p.v("tensor_tensor", a_ap[:, :, 0, :, :],
                        a_ap[:, :, 0, :, :], a_ap[:, :, 1, :, :],
                        op=AL.add, waits=[tprev], inc=True)
            tdiff = p.v("scalar_tensor_tensor", a_ap[:, :, 1, :, :],
                        a_ap[:, :, 1, :, :], -2.0, a_ap[:, :, 0, :, :],
                        op0=AL.mult, op1=AL.add, waits=[tsum_], inc=True)
            tprev = tdiff
            hh *= 2
        if stop_after == "l1f_fw" and half == 0:
            finish([tprev] * NT)
            build_blocks(nc, p)
            es.close()
            return nc
        tcm = p.v("tensor_reduce", fsc[:, :],
                  bufH[:, :].rearrange("p (ch t) -> p t ch", t=512),
                  axis=AX.X, op=AL.max, apply_absolute_value=True,
                  waits=[tprev, t_fsc_f], inc=True)
        tm2 = {}
        for sub in range(4):
            ttr = p.tr(pp[2][:, 0:128], fsc[:, sub * 128:(sub + 1) * 128],
                       IDENTF, waits=[tcm] + bw(2), inc=True)
            tm2[sub] = p.v("reduce_max", sml[:, 50:51], pp[2][:, 0:128],
                           axis=AX.X, waits=[ttr], inc=True)
            bank_free[2] = tm2[sub]
            j = half * 4 + sub
            trc_ = p.v("tensor_scalar", sml[:, 51:52], INVS(j), ISW(1),
                       FWHT_C, op0=AL.mult, op1=AL.mult, waits=[tm2[sub]],
                       inc=True)
            tmt = p.v("tensor_tensor", sml[:, 52:53], sml[:, 50:51],
                      sml[:, 51:52], op=AL.mult, waits=[trc_], inc=True)
            tc2 = p.v("tensor_scalar", sml[:, 53:54], sml[:, 52:53], 1e-5,
                      None, op0=AL.max, waits=[tmt], inc=True)
            td_ = p.v("reciprocal", sml[:, 54:55], sml[:, 53:54],
                      waits=[tc2], inc=True)
            tse = p.v("tensor_scalar", sml[:, 68 + sub:69 + sub],
                      sml[:, 54:55], sml[:, 51:52], 128.0, op0=AL.mult,
                      op1=AL.mult, waits=[td_], inc=True)
            tes = p.v("tensor_scalar", sml[:, 72 + sub:73 + sub],
                      sml[:, 53:54], ISW(2), 1.0 / 128.0, op0=AL.mult,
                      op1=AL.mult, waits=[tc2], inc=True)
        trow2 = []
        for sub in range(4):
            ttr = p.tr(pp[2][0:1, 0:128], sml[:, 68 + sub:69 + sub], IDENTF,
                       waits=[tse, tes] + bw(2), inc=True)
            tcp = p.v("tensor_copy", fsc[0:1, sub * 128:(sub + 1) * 128],
                      pp[2][0:1, 0:128], waits=[ttr, tcm], inc=True)
            bank_free[2] = tcp
            trow2.append(tcp)
        t_seb = p.mm(pp[4][:, 0:512], ONESROW, fsc[0:1, 0:512], True, True,
                     waits=trow2 + bw(4), inc=True)
        trow3 = []
        for sub in range(4):
            ttr = p.tr(pp[2][0:1, 0:128], sml[:, 72 + sub:73 + sub], IDENTF,
                       waits=[t_seb] + bw(2), inc=True)
            tcp = p.v("tensor_copy", fsc[0:1, sub * 128:(sub + 1) * 128],
                      pp[2][0:1, 0:128], waits=[ttr] + trow2, inc=True)
            bank_free[2] = tcp
            trow3.append(tcp)
        t_esb = p.mm(pp[5][:, 0:512], ONESROW, fsc[0:1, 0:512], True, True,
                     waits=trow3 + bw(5), inc=True)
        t_fsc_f = t_esb
        if stop_after == "l1f_se" and half == 0:
            finish([t_esb] * NT)
            build_blocks(nc, p)
            es.close()
            return nc
        t_q2 = {}
        for i in range(64):
            hsl = slice(i * 512, (i + 1) * 512)
            tu = p.v("tensor_tensor", fsc[:, :], bufH[:, hsl],
                     pp[4][:, 0:512], op=AL.mult,
                     waits=[t_seb, tprev, t_fsc_f], inc=True)
            tr1_ = p.v("tensor_scalar", fsc[:, :], fsc[:, :], MAGIC, None,
                       op0=AL.add, waits=[tu], inc=True)
            tr2_ = p.v("tensor_scalar", fsc[:, :], fsc[:, :], MAGIC, 127.0,
                       op0=AL.subtract, op1=AL.min, waits=[tr1_], inc=True)
            tr3_ = p.v("tensor_scalar", bufH[:, hsl], fsc[:, :], -128.0,
                       None, op0=AL.max, waits=[tr2_], inc=True)
            t_q2[i] = tr3_
            t_fsc_f = tr3_
        if stop_after == "l1f_q2" and half == 0:
            finish([t_q2[63]] * NT)
            build_blocks(nc, p)
            es.close()
            return nc
        for og2 in range(4):
            tmm4 = [None] * 4
            for kch in range(64):
                sl_ = kch % 4
                blk = (kch // 8) * 32 + og2 * 8 + (kch % 8)
                twl = p.dma(["sync", "scalar"][kch % 2],
                            bk[:, sl_ * 512:(sl_ + 1) * 512],
                            dt["ago_wd"][blk, :, :],
                            waits=([slot_free[sl_]] if slot_free[sl_]
                                   else [(SLOT_SEM[sl_], 0)]),
                            sem=SLOT_SEM[sl_])
                for c in range(4):
                    tmm4[c] = p.mm(
                        pp[c][:, 0:512],
                        bk[:, sl_ * 512 + c * 128:sl_ * 512 + c * 128 + 128],
                        bufH[:, kch * 512:(kch + 1) * 512],
                        start=(kch == 0), stop=(kch == 63),
                        waits=[twl, t_q2[kch]] + (bw(c) if kch == 0 else []),
                        inc=(c == 3))
                slot_free[sl_] = tmm4[3]
            for c in range(4):
                ot = og2 * 4 + c
                tyc = p.v("tensor_copy", fsc[:, :], pp[c][:, 0:512],
                          waits=[tmm4[3], t_fsc_f], inc=True)
                t_fsc_f = tyc
                ty = p.v("tensor_tensor", bfs[:, 0:512], fsc[:, :],
                         pp[5][:, 0:512], op=AL.mult,
                         waits=[tyc, t_esb, t_bfs_f], inc=True)
                bank_free[c] = ty
                for sub in range(4):
                    j = half * 4 + sub
                    bi = "b%d" % (sub % 2)
                    ttr = p.tr(ppb[sub % 2][:, 0:128],
                               bfs[:, sub * 128:(sub + 1) * 128], IDENT,
                               waits=[ty] + bw(bi), inc=True)
                    tres = p.v("tensor_tensor",
                               strm[:, j * D + ot * 128:j * D + ot * 128 + 128],
                               strm[:, j * D + ot * 128:j * D + ot * 128 + 128],
                               ppb[sub % 2][:, 0:128], op=AL.add,
                               waits=[ttr] + ([res4[(j, ot)]] if (j, ot) in res4
                                              else []),
                               inc=True)
                    res4[(j, ot)] = tres
                    bank_free[bi] = tres
                t_bfs_f = ttr
        t_bk_ffn = slot_free[0]
        t_bv_ffn = slot_free[1]
        bank_free[5] = ty
        t_A_f = ("tt", p.cnt["tt"])
        if stop_after == "l1f_dn" and half == 0:
            finish([res4[(3, 15)]] * NT)
            build_blocks(nc, p)
            es.close()
            return nc

    ln4 = layernorm_pass([res4[(j, 15)] for j in range(NT)], make_fm=False)
    finish(ln4)
    build_blocks(nc, p)
    es.close()
    return nc


def build_blocks(nc, p):
    with nc.Block() as block:
        @block.sync
        def _(eng):
            for op in p.ops["sync"]:
                op(eng)

        @block.scalar
        def _(eng):
            for op in p.ops["scalar"]:
                op(eng)

        @block.vector
        def _(eng):
            for op in p.ops["vector"]:
                op(eng)

        @block.tensor
        def _(eng):
            for op in p.ops["tensor"]:
                op(eng)

        @block.gpsimd
        def _(eng):
            for op in p.ops["gpsimd"]:
                op(eng)



def hadamard(n):
    h = np.ones((1, 1), np.float32)
    while h.shape[0] < n:
        h = np.block([[h, h], [h, -h]])
    return h.astype(np.float32)


def make_inputs(inputs):
    import ml_dtypes
    bf = ml_dtypes.bfloat16
    f32 = np.float32

    x = np.asarray(inputs["L0_attn_x"], f32)
    att0 = np.concatenate([np.ascontiguousarray(np.asarray(inputs[k], f32).T)
                           for k in ("L0_attn_wq", "L0_attn_wk",
                                     "L0_attn_wv", "L0_attn_wo")], axis=0)
    gu_t = np.ascontiguousarray(np.asarray(inputs["L0_ffn_w_gate_up"], f32).T)
    proj_t = np.ascontiguousarray(np.asarray(inputs["L0_ffn_w_proj"], f32).T)
    att1 = np.concatenate([np.ascontiguousarray(np.asarray(inputs[k], f32).T)
                           for k in ("L1_attn_Wq", "L1_attn_Wk",
                                     "L1_attn_Wv", "L1_attn_Wo")], axis=0)
    perm = (np.arange(FF_H) % 128) * 64 + np.arange(FF_H) // 128
    wg_t = np.ascontiguousarray(np.asarray(inputs["L1_ffn_w_gate"], f32).T[:, perm])
    wu_t = np.ascontiguousarray(np.asarray(inputs["L1_ffn_w_up"], f32).T[:, perm])
    wd_t = np.ascontiguousarray(np.asarray(inputs["L1_ffn_w_down"], f32).T[perm, :])

    fbias = np.stack([np.asarray(inputs["L0_attn_fq1_b"], f32),
                      np.asarray(inputs["L0_attn_fq2_b"], f32),
                      np.asarray(inputs["L0_attn_fk1_b"], f32),
                      np.asarray(inputs["L0_attn_fk2_b"], f32)], axis=1)
    normw = np.asarray(inputs["L0_attn_norm_w"], f32).reshape(128, 1)
    l1bias_pp = np.zeros((128, 64), f32)
    for m, k in enumerate(("L1_attn_bq", "L1_attn_bk", "L1_attn_bv",
                           "L1_attn_bo")):
        l1bias_pp[:, m * 16:(m + 1) * 16] = np.asarray(
            inputs[k], f32).reshape(16, 128).T
    borow = np.asarray(inputs["L1_attn_bo"], f32).reshape(1, D)

    h128 = hadamard(128).astype(bf)
    b2 = np.zeros((2, 128), f32)
    b2[0, :64] = 1.0
    b2[1, 64:] = 1.0

    half = DK // 2
    inv_freq = (1.0 / 10000.0 ** (np.arange(half, dtype=f32)
                                  / f32(half))).astype(f32)

    cval_arr = np.tile(np.array(CONST_VALS, f32)[None, :], (128, 1))
    in_maps = []
    for c in range(NC_):
        seq, r = c // GRP, c % GRP
        tiles = [r + 4 * i for i in range(NT)]
        rows = np.concatenate([np.arange(g * 128, (g + 1) * 128)
                               for g in tiles])
        xs = np.ascontiguousarray(x[seq].reshape(SEQ, D)[rows])
        positions = rows.astype(f32)
        th = positions[:, None] * inv_freq[None, :]
        cos = np.repeat(np.cos(th), 2, axis=-1).astype(f32)
        sin = np.repeat(np.sin(th), 2, axis=-1).astype(f32)
        cos_t = np.ascontiguousarray(np.tile(cos, (1, NH)))
        sin_t = np.ascontiguousarray(np.tile(sin, (1, NH)))
        qr = np.arange(128)[:, None]
        kc = np.arange(512)[None, :]
        diagmask = (kc <= r * 128 + qr).astype(f32).astype(bf)
        partflag = np.zeros((128, 3), f32)
        for rp in range(3):
            partflag[:, rp] = 1.0 if rp < r else 0.0

        m = {
            "x": xs,
            "att0_s": np.ascontiguousarray(
                att0.reshape(4, D, D)[:, c * 256:(c + 1) * 256, :]
                .reshape(1024, D)),
            "fq1t": np.ascontiguousarray(np.asarray(inputs["L0_attn_fq1_w"], f32).T),
            "fq2t": np.ascontiguousarray(np.asarray(inputs["L0_attn_fq2_w"], f32).T),
            "fk1t": np.ascontiguousarray(np.asarray(inputs["L0_attn_fk1_w"], f32).T),
            "fk2t": np.ascontiguousarray(np.asarray(inputs["L0_attn_fk2_w"], f32).T),
            "fbias": fbias,
            "normw": normw,
            "gu_s": np.ascontiguousarray(gu_t[c * 256:(c + 1) * 256, :]),
            "proj_s": np.ascontiguousarray(proj_t[c * 1024:(c + 1) * 1024, :]),
            "att1_s": np.ascontiguousarray(
                att1.reshape(4, D, D)[:, c * 256:(c + 1) * 256, :]
                .reshape(1024, D)),
            "l1bias_pp": l1bias_pp,
            "borow": borow,
            "wg_s": np.ascontiguousarray(wg_t[c * 256:(c + 1) * 256, :]),
            "wu_s": np.ascontiguousarray(wu_t[c * 256:(c + 1) * 256, :]),
            "wd_s": np.ascontiguousarray(wd_t[c * 1024:(c + 1) * 1024, :]),
            "cos_t": cos_t.astype(bf),
            "sin_t": sin_t.astype(bf),
            "ident": np.eye(128, dtype=f32).astype(bf),
            "identf": np.eye(128, dtype=f32),
            "h128c": h128,
            "b2c": b2.astype(bf),
            "b2tc": np.ascontiguousarray(b2.T).astype(bf),
            "ones64": np.ones((64, 1), f32).astype(bf),
            "ones128": np.ones((128, 1), f32).astype(bf),
            "onesrow_f32": np.ones((1, 128), f32),
            "tri64": (np.arange(64)[:, None] <= np.arange(64)[None, :])
                     .astype(f32).astype(bf),
            "diagmask": diagmask,
            "partflag": partflag,
            "cvals": cval_arr,
            "cvalsb": cval_arr.astype(bf),
        }
        in_maps.append(m)
    return in_maps


def assemble(results):
    f32 = np.float32
    out = np.empty((B_, SEQ, D), f32)
    for c in range(NC_):
        seq, r = c // GRP, c % GRP
        o = np.asarray(results[c]["out"], f32)
        for i in range(NT):
            g = r + 4 * i
            out[seq, g * 128:(g + 1) * 128, :] = o[i * 128:(i + 1) * 128, :]
    return out


def kernel(**inputs):
    in_maps = make_inputs(inputs)
    nc = build_program()
    res = run_bass_kernel_spmd(nc, in_maps, core_ids=list(range(NC_)))
    return assemble(res.results)


# version 04
# speedup vs baseline: 1.1470x; 1.1470x over previous
import numpy as np

import concourse.bass as bass
import concourse.mybir as mybir
from concourse.bass_utils import run_bass_kernel_spmd

F32 = mybir.dt.float32
BF16 = mybir.dt.bfloat16
AL = mybir.AluOpType
AF = mybir.ActivationFunctionType
AX = mybir.AxisListType

MAGIC = float(np.float32(12582912.0))
F32EPS = float(np.finfo(np.float32).eps)
DKSCALE = float(np.float32(128.0 ** -0.5))
ATTSCALE = float(np.float32(1.0) / np.float32(np.sqrt(np.float32(128.0))))
FWHT_C = float(np.float32(1.0 / np.sqrt(8192.0)))

B_, SEQ, D = 2, 4096, 2048
NH, DK = 16, 128
TLOC, NT = 1024, 8
GU_O, FF_H = 16384, 8192
NC_, GRP = 8, 4

ENGINES = ("sync", "scalar", "vector", "tensor", "gpsimd")

CONST_VALS = [1e-5, 1.0 / 128.0, MAGIC, 127.0, -128.0, 1.0, -1.0, -2.0,
              1.0 / 64.0, DKSCALE, 1e-8, 1.0 / 8192.0, F32EPS, 128.0,
              1.0 / 16777216.0, ATTSCALE, FWHT_C, 0.0, 0.5, 2.0]

STOP_AFTER = "all"


class Prog:
    def __init__(self, nc):
        self.nc = nc
        self.ops = {e: [] for e in ENGINES}
        self.sems = {}
        self.cnt = {}

    def emit(self, eng, fname, *args, waits=(), inc=None, inc_n=1, **kw):
        tok = None
        if inc is not None:
            self.cnt[inc] += inc_n
            tok = (inc, self.cnt[inc])
        wl = tuple((s, (self.cnt[s] if s.startswith("d") else v))
                   for (s, v) in (w for w in waits if w is not None) if v > 0)
        sems = self.sems

        def op(e, _f=fname, _a=args, _k=kw, _w=wl, _i=inc, _n=inc_n):
            for s, v in _w:
                e.wait_ge(sems[s], v)
            r = getattr(e, _f)(*_a, **_k)
            if _i is not None:
                r.then_inc(sems[_i], _n)

        self.ops[eng].append(op)
        return tok

    def wait(self, eng, waits):
        wl = tuple((s, (self.cnt[s] if s.startswith("d") else v))
                   for (s, v) in (w for w in waits if w is not None) if v > 0)
        if not wl:
            return
        sems = self.sems

        def op(e, _w=wl):
            for s, v in _w:
                e.wait_ge(sems[s], v)

        self.ops[eng].append(op)

    def dma(self, q, out, in_, waits=(), sem=None):
        assert sem is not None
        return self.emit(q, "dma_start", waits=waits, inc=sem, inc_n=16,
                         out=out, in_=in_)

    def mm(self, out, lhsT, rhs, start, stop, waits=(), inc=False):
        return self.emit("tensor", "matmul", out, lhsT, rhs, waits=waits,
                         inc=("tt" if inc else None), start=start, stop=stop,
                         skip_group_check=True)

    def tr(self, out, in_, ident, waits=(), inc=False):
        return self.emit("tensor", "transpose", out, in_, ident, waits=waits,
                         inc=("tt" if inc else None))

    def v(self, fname, *args, waits=(), inc=False, **kw):
        return self.emit("vector", fname, *args, waits=waits,
                         inc=("vt" if inc else None), **kw)

    def s(self, fname, *args, waits=(), inc=False, **kw):
        return self.emit("scalar", fname, *args, waits=waits,
                         inc=("st" if inc else None), **kw)

    def g(self, fname, *args, waits=(), inc=False, **kw):
        return self.emit("gpsimd", fname, *args, waits=waits,
                         inc=("gt" if inc else None), **kw)



def build_program(stop_after=STOP_AFTER, debug=False):
    from contextlib import ExitStack
    nc = bass.Bass("TRN2", target_bir_lowering=False, debug=debug)
    p = Prog(nc)
    dt = {}

    def din(name, shape, dtype):
        dt[name] = nc.dram_tensor(name, list(shape), dtype,
                                  kind="ExternalInput").ap()

    def dint(name, shape, dtype):
        dt[name] = nc.dram_tensor(name, list(shape), dtype).ap()

    din("x", (TLOC, D), F32)
    din("att0_s", (1024, D), F32)
    din("fq1t", (128, 128), F32)
    din("fq2t", (128, 128), F32)
    din("fk1t", (128, 128), F32)
    din("fk2t", (128, 128), F32)
    din("fbias", (128, 4), F32)
    din("normw", (128, 1), F32)
    din("gu_s", (256, GU_O), F32)
    din("proj_s", (1024, D), F32)
    din("att1_s", (1024, D), F32)
    din("l1bias_pp", (128, 64), F32)
    din("borow", (1, D), F32)
    din("wg_s", (256, FF_H), F32)
    din("wu_s", (256, FF_H), F32)
    din("wd_s", (1024, D), F32)
    din("cos_t", (TLOC, D), BF16)
    din("sin_t", (TLOC, D), BF16)
    din("ident", (128, 128), BF16)
    din("identf", (128, 128), F32)
    din("h128c", (128, 128), BF16)
    din("b2c", (2, 128), BF16)
    din("b2tc", (128, 2), BF16)
    din("ones64", (64, 1), BF16)
    din("ones128", (128, 1), BF16)
    din("onesrow_f32", (1, 128), F32)
    din("tri64", (64, 64), BF16)
    din("diagmask", (128, 512), BF16)
    din("partflag", (128, 3), F32)
    din("cvals", (128, len(CONST_VALS)), F32)
    din("cvalsb", (128, len(CONST_VALS)), BF16)
    out_ap = nc.dram_tensor("out", [TLOC, D], F32, kind="ExternalOutput").ap()
    dbg = {}
    if stop_after == "dbg0":
        dbg["qm"] = nc.dram_tensor("dbg_qm", [128, 16 * 1024], BF16,
                                   kind="ExternalOutput").ap()
        dbg["km"] = nc.dram_tensor("dbg_km", [128, D], BF16,
                                   kind="ExternalOutput").ap()
        dbg["vc"] = nc.dram_tensor("dbg_vc", [128, D], BF16,
                                   kind="ExternalOutput").ap()
        dbg["kv"] = nc.dram_tensor("dbg_kv", [NH * 128, 128], BF16,
                                   kind="ExternalOutput").ap()
        dbg["ofm"] = nc.dram_tensor("dbg_ofm", [128, D], BF16,
                                    kind="ExternalOutput").ap()
        dbg["pre"] = nc.dram_tensor("dbg_pre", [TLOC, D], F32,
                                    kind="ExternalOutput").ap()

    dint("agi_att0", (32, 128, 512), BF16)
    dint("ago_att0", (256, 128, 512), BF16)
    dint("agi_gu", (64, 128, 512), BF16)
    dint("ago_gu", (512, 128, 512), BF16)
    dint("agi_proj", (32, 128, 512), BF16)
    dint("ago_proj", (256, 128, 512), BF16)
    dint("agi_att1", (32, 128, 512), BF16)
    dint("ago_att1", (256, 128, 512), BF16)
    dint("agi_wg", (32, 128, 512), BF16)
    dint("ago_wg", (256, 128, 512), BF16)
    dint("agi_wu", (32, 128, 512), BF16)
    dint("ago_wu", (256, 128, 512), BF16)
    dint("agi_wd", (32, 128, 512), BF16)
    dint("ago_wd", (256, 128, 512), BF16)
    dint("agi_sums", (1, 4), F32)
    dint("ago_sums", (1, 4), F32)
    dint("agi_kv0", (NT * NH * 128, 128), BF16)
    dint("ago_kv0", (4 * NT * NH * 128, 128), BF16)
    dint("agi_kv1", (256, 128, 128), BF16)
    dint("ago_kv1", (1024, 128, 128), BF16)
    dint("km_d", (NT, 128, D), BF16)
    dint("vc_d", (NT, 128, D), BF16)
    dint("ktm_d", (NT, 128, D), BF16)
    dint("qtm_d", (NT, 128, D), BF16)
    dint("ofm_d", (NT * 16, 128, 128), BF16)

    es = ExitStack()
    sb = {}

    def sbuf(name, shape, dtype):
        sb[name] = es.enter_context(nc.sbuf_tensor(name, list(shape), dtype))

    def psum(name, shape=(128, 512)):
        sb[name] = es.enter_context(nc.psum_tensor(name, list(shape), F32))

    sbuf("strm", (128, NT * D), F32)
    sbuf("bufA", (128, 16 * 1024), BF16)
    sbuf("bufH", (128, 64 * 512), BF16)
    sbuf("xfmh", (128, 16 * 256), BF16)
    sbuf("scS", (128, 6144), BF16)
    sbuf("fsc", (128, 512), F32)
    sbuf("bk", (128, 2048), BF16)
    sbuf("bv", (128, 2048), BF16)
    sbuf("bfs", (128, 2048), BF16)
    sbuf("wb0", (128, 512), BF16)
    sbuf("wb1", (128, 512), BF16)
    sbuf("ab0", (128, 128), BF16)
    sbuf("ab1", (128, 128), BF16)
    sbuf("sml", (128, 320), F32)
    sbuf("smlb", (128, 256), BF16)
    sbuf("cstb", (128, 840), BF16)
    sbuf("l1b", (128, 64), F32)
    sbuf("cstf", (128, 400), F32)
    sbuf("c2r", (2, 128), BF16)
    sbuf("r1", (1, 384), F32)
    sbuf("cvf", (128, len(CONST_VALS)), F32)
    sbuf("cvb", (128, len(CONST_VALS)), BF16)
    for i in range(6):
        psum("pp%d" % i)
    sb["ppb0"] = es.enter_context(nc.psum_tensor("ppb0", [128, 512], BF16))
    sb["ppb1"] = es.enter_context(nc.psum_tensor("ppb1", [128, 512], BF16))
    for i, cval in enumerate(CONST_VALS):
        nc.const_aps.register(F32, cval, sb["cvf"][:, i:i + 1])
        nc.const_aps.register(BF16, cval, sb["cvb"][:, i:i + 1])

    for s_ in ("dx", "dcs", "dfq", "dag", "dkm", "dbk", "dbv", "dsc",
               "dwb0", "dwb1", "dab0", "dab1", "dwo", "dof", "ddbg", "dout",
               "dk1", "dk2", "dk3", "dk4",
               "tt", "vt", "st", "gt", "cc"):
        p.sems[s_] = es.enter_context(nc.semaphore(s_))
        p.cnt[s_] = 0

    strm, bufA, bufH = sb["strm"], sb["bufA"], sb["bufH"]
    xfmh, scS = sb["xfmh"], sb["scS"]
    bk, bv, bfs = sb["bk"], sb["bv"], sb["bfs"]
    wb = [sb["wb0"], sb["wb1"]]
    ab = [sb["ab0"], sb["ab1"]]
    sml, smlb = sb["sml"], sb["smlb"]
    cstb, cstf, c2r, r1 = sb["cstb"], sb["cstf"], sb["c2r"], sb["r1"]
    pp = [sb["pp%d" % i] for i in range(6)]
    ppb = [sb["ppb0"], sb["ppb1"]]

    fsc = sb["fsc"]

    IDENT = cstb[:, 0:128]
    H128T = cstb[:, 128:256]
    TRI64 = cstb[0:64, 256:320]
    DIAG = cstb[:, 320:832]
    ONES64 = cstb[0:64, 832:833]
    ONES128 = cstb[:, 833:834]
    IDENTF = cstf[:, 0:128]
    PFLAG = lambda r: cstf[:, 128 + r:129 + r]
    FB = lambda i: cstf[:, 131 + i:132 + i]
    NORMW = cstf[:, 135:136]
    FK1BC = cstf[:, 136:264]
    FK2BC = cstf[:, 264:392]
    FB1S = cstf[:, 392:393]
    ONESROW = r1[0:1, 256:384]

    tx = [p.dma("sync", strm[:, j * D:(j + 1) * D],
                 dt["x"][j * 128:(j + 1) * 128, :], sem="dx")
          for j in range(NT)]
    p.dma("gpsimd", sb["cvf"][:, :], dt["cvals"][:, :], sem="dcs")
    p.dma("gpsimd", sb["cvb"][:, :], dt["cvalsb"][:, :], sem="dcs")
    p.dma("gpsimd", cstb[:, 0:128], dt["ident"][:, :], sem="dcs")
    p.dma("gpsimd", cstb[:, 128:256], dt["h128c"][:, :], sem="dcs")
    p.dma("gpsimd", cstb[0:64, 256:320], dt["tri64"][:, :], sem="dcs")
    p.dma("gpsimd", cstb[:, 320:832], dt["diagmask"][:, :], sem="dcs")
    p.dma("gpsimd", cstb[0:64, 832:833], dt["ones64"][:, :], sem="dcs")
    p.dma("gpsimd", cstb[:, 833:834], dt["ones128"][:, :], sem="dcs")
    p.dma("gpsimd", cstf[:, 0:128], dt["identf"][:, :], sem="dcs")
    p.dma("gpsimd", cstf[:, 128:131], dt["partflag"][:, :], sem="dcs")
    p.dma("gpsimd", sb["l1b"][:, :], dt["l1bias_pp"][:, :], sem="dcs")
    p.dma("gpsimd", cstf[:, 131:135], dt["fbias"][:, :], sem="dcs")
    p.dma("gpsimd", cstf[:, 135:136], dt["normw"][:, :], sem="dcs")
    p.dma("gpsimd", r1[0:1, 256:384], dt["onesrow_f32"][:, :], sem="dcs")
    p.dma("gpsimd", c2r[:, 0:128], dt["b2c"][:, :], sem="dcs")
    tconst = p.dma("gpsimd", cstb[:, 834:836], dt["b2tc"][:, :], sem="dcs")

    for _e in ("vector", "scalar", "tensor", "gpsimd"):
        p.wait(_e, [tconst])

    FQ1, FQ2 = bufH[:, 0:128], bufH[:, 128:256]
    FK1, FK2 = bufH[:, 256:384], bufH[:, 384:512]
    p.dma("sync", fsc[:, 0:128], dt["fq1t"][:, :], sem="dfq")
    p.dma("sync", fsc[:, 128:256], dt["fq2t"][:, :], sem="dfq")
    p.dma("sync", fsc[:, 256:384], dt["fk1t"][:, :], sem="dfq")
    tq = p.dma("sync", fsc[:, 384:512], dt["fk2t"][:, :], sem="dfq")
    p.v("tensor_scalar", FQ1, fsc[:, 0:128], DKSCALE, None, op0=AL.mult,
        waits=[tq])
    p.v("tensor_copy", FQ2, fsc[:, 128:256], waits=[tq])
    p.v("tensor_copy", FK1, fsc[:, 256:384], waits=[tq])
    tfq = p.v("tensor_copy", FK2, fsc[:, 384:512], waits=[tq], inc=True)
    p.v("tensor_scalar", FB1S, FB(0), DKSCALE, None, op0=AL.mult,
        waits=[tconst])
    p.wait("tensor", [tconst])
    t1 = p.tr(pp[0][0:1, 0:128], FB(2), IDENTF, inc=True)
    t2 = p.tr(pp[1][0:1, 0:128], FB(3), IDENTF, inc=True)
    tr1 = p.v("tensor_copy", r1[0:1, 0:128], pp[0][0:1, 0:128], waits=[t1],
              inc=True)
    tr2 = p.v("tensor_copy", r1[0:1, 128:256], pp[1][0:1, 0:128], waits=[t2],
              inc=True)
    tb1 = p.mm(pp[2][:, 0:128], ONESROW, r1[0:1, 0:128], True, True,
               waits=[tr1], inc=True)
    tb2 = p.mm(pp[3][:, 0:128], ONESROW, r1[0:1, 128:256], True, True,
               waits=[tr2], inc=True)
    tk1 = p.v("tensor_copy", FK1BC, pp[2][:, 0:128], waits=[tb1], inc=True)
    tk2 = p.v("tensor_copy", FK2BC, pp[3][:, 0:128], waits=[tb2], inc=True)

    bank_free = {i: None for i in range(6)}
    bank_free["b0"] = None
    bank_free["b1"] = None
    bank_free[0] = tr1
    bank_free[1] = tr2
    bank_free[2] = tk1
    bank_free[3] = tk2

    def bw(i):
        t = bank_free[i]
        return [t] if t else []

    t_fsc_free = None
    ag_last = None
    for m in range(4):
        for kc in range(2):
            tcast = None
            for q4 in range(4):
                tl = p.dma("scalar", fsc[:, :],
                           dt["att0_s"][(m * 2 + kc) * 128:
                                        (m * 2 + kc + 1) * 128,
                                        q4 * 512:(q4 + 1) * 512],
                           waits=([t_fsc_free] if t_fsc_free else []) + [tfq],
                           sem="dk1")
                wprev = [ag_last] if (ag_last and q4 == 0) else []
                if m == 3:
                    tcast = p.v("tensor_scalar",
                                bfs[:, q4 * 512:(q4 + 1) * 512], fsc[:, :],
                                NORMW, None, op0=AL.mult,
                                waits=[tl, tconst] + wprev, inc=True)
                else:
                    tcast = p.v("tensor_copy",
                                bfs[:, q4 * 512:(q4 + 1) * 512], fsc[:, :],
                                waits=[tl] + wprev, inc=True)
                t_fsc_free = tcast
            for ocg in range(4):
                blk0 = (m * 4 + ocg) * 2 + kc
                ag_last = p.dma("gpsimd", dt["agi_att0"][blk0, :, :],
                                bfs[:, ocg * 512:(ocg + 1) * 512],
                                waits=[tcast], sem="dag")
    p.wait("gpsimd", [ag_last])
    cc_att0 = p.emit("gpsimd", "collective_compute", "AllGather", AL.bypass,
                     replica_groups=[list(range(8))],
                     ins=[dt["agi_att0"].opt()], outs=[dt["ago_att0"].opt()],
                     inc="cc")
    t_piece_free = {0: ag_last, 1: ag_last}

    def att0_blk(m, kch, ocg):
        return (kch // 2) * 32 + (m * 4 + ocg) * 2 + (kch % 2)

    tA_q = {}
    t_km = {}
    t_vc = {}
    t_sml_free = ("vt", 0)
    t_xfm_texture = None
    wb_free = [None, None]
    t_bfs_free = ag_last
    t_fsc2_free = None
    p.wait("tensor", [cc_att0, tfq, tx[-1]])
    XFM = bufH[:, 16384:24576]
    SLOT_SEM = ["dbk", "dbv", "dk2", "dk3"]
    slot_free = [None, None, None, None]
    t_xfm_read = None
    for half in range(2):
        txf_last = None
        for j4 in range(4):
            j = half * 4 + j4
            tcastx = p.s("copy", bfs[:, :], strm[:, j * D:(j + 1) * D],
                         waits=[tx[j], t_piece_free[0], t_piece_free[1]]
                         + ([t_bfs_free] if t_bfs_free else []),
                         inc=True)
            for ch in range(16):
                bi = "b%d" % (ch % 2)
                pb_ = ppb[ch % 2]
                ttr = p.tr(pb_[:, 0:128], bfs[:, ch * 128:(ch + 1) * 128],
                           IDENT, waits=[tcastx] + bw(bi), inc=True)
                tev = p.v("tensor_copy",
                          XFM[:, ch * 512 + j4 * 128:ch * 512 + j4 * 128 + 128],
                          pb_[:, 0:128],
                          waits=[ttr] + ([t_xfm_read] if (j4 == 0 and ch == 0
                                                          and t_xfm_read)
                                         else []),
                          inc=True)
                bank_free[bi] = tev
                txf_last = tev
            t_bfs_free = ttr
        t_xfm_ready = txf_last

        for m in range(3):
            for ocg in range(4):
                tmm4 = [None] * 4
                for kch in range(16):
                    sl_ = kch % 4
                    blk = att0_blk(m, kch, ocg)
                    twl = p.dma(["sync", "scalar"][kch % 2],
                                bk[:, sl_ * 512:(sl_ + 1) * 512],
                                dt["ago_att0"][blk, :, :],
                                waits=([slot_free[sl_]] if slot_free[sl_]
                                       else [(SLOT_SEM[sl_], 0), cc_att0]),
                                sem=SLOT_SEM[sl_])
                    for c in range(4):
                        tmm4[c] = p.mm(
                            pp[2 + c][:, 0:512],
                            bk[:, sl_ * 512 + c * 128:sl_ * 512 + c * 128 + 128],
                            XFM[:, kch * 512:(kch + 1) * 512],
                            start=(kch == 0), stop=(kch == 15),
                            waits=[twl, t_xfm_ready]
                            + (bw(2 + c) if kch == 0 else []),
                            inc=(c == 3))
                    slot_free[sl_] = tmm4[3]
                for c in range(4):
                    h = ocg * 4 + c
                    te = p.s("copy", bfs[:, 0:512], pp[2 + c][:, 0:512],
                             waits=[tmm4[3]]
                             + ([t_bfs_free] if t_bfs_free else []),
                             inc=True)
                    bank_free[2 + c] = te
                    if m == 0:
                        tm1 = p.mm(pp[0][:, 0:512], FQ1, bfs[:, 0:512],
                                   True, True, waits=[te] + bw(0), inc=True)
                        tm2 = p.mm(pp[1][:, 0:512], FQ2, bfs[:, 0:512],
                                   True, True, waits=[te] + bw(1), inc=True)
                        te2 = p.v("tensor_scalar", bfs[:, 512:1024],
                                  pp[1][:, 0:512], FB(1), None, op0=AL.add,
                                  waits=[tm2, t_piece_free[0], t_piece_free[1]],
                                  inc=True)
                        t_piece_free = {0: te2, 1: te2}
                        tqm = p.v("scalar_tensor_tensor",
                                  bufA[:, h * 1024 + half * 512:
                                       h * 1024 + half * 512 + 512],
                                  pp[0][:, 0:512], FB1S, bfs[:, 512:1024],
                                  op0=AL.add, op1=AL.mult,
                                  waits=[tm1, te2, tk1], inc=True)
                        tA_q[(h, half * 2)] = tqm
                        tA_q[(h, half * 2 + 1)] = tqm
                        bank_free[0] = tqm
                        bank_free[1] = te2
                        t_bfs_free = tqm
                    elif m == 1:
                        t_last = None
                        for sub in range(4):
                            j = half * 4 + sub
                            a_ap = bfs[:, sub * 128:(sub + 1) * 128]
                            tm1 = p.mm(pp[0][:, 0:128], a_ap, FK1,
                                       True, True, waits=[te] + bw(0),
                                       inc=True)
                            tm2 = p.mm(pp[1][:, 0:128], a_ap, FK2,
                                       True, True, waits=[te] + bw(1),
                                       inc=True)
                            tv1 = p.v("tensor_tensor", sml[:, 64:192],
                                      pp[0][:, 0:128], FK1BC, op=AL.add,
                                      waits=[tm1, tk1, t_sml_free], inc=True)
                            tv2 = p.v("tensor_tensor", sml[:, 192:320],
                                      pp[1][:, 0:128], FK2BC, op=AL.add,
                                      waits=[tm2, tk2, t_sml_free], inc=True)
                            tkm = p.v("tensor_tensor",
                                      bfs[:, 512 + (sub % 2) * 128:
                                           512 + (sub % 2) * 128 + 128],
                                      sml[:, 64:192], sml[:, 192:320],
                                      op=AL.mult,
                                      waits=[tv1, tv2, t_piece_free[sub % 2]],
                                      inc=True)
                            t_sml_free = tkm
                            bank_free[0] = tv1
                            bank_free[1] = tv2
                            t_km[(j, h)] = p.dma(
                                "gpsimd",
                                dt["km_d"][j, :, h * 128:(h + 1) * 128],
                                bfs[:, 512 + (sub % 2) * 128:
                                     512 + (sub % 2) * 128 + 128],
                                waits=[tkm], sem="dkm")
                            t_piece_free[sub % 2] = t_km[(j, h)]
                            t_last = tkm
                        t_bfs_free = t_last
                    else:
                        t_last = None
                        for sub in range(4):
                            j = half * 4 + sub
                            bi = "b%d" % (sub % 2)
                            ttr = p.tr(ppb[sub % 2][:, 0:128],
                                       bfs[:, sub * 128:(sub + 1) * 128],
                                       IDENT, waits=[te] + bw(bi), inc=True)
                            tev = p.v("tensor_copy",
                                      bfs[:, 512 + (sub % 2) * 128:
                                           512 + (sub % 2) * 128 + 128],
                                      ppb[sub % 2][:, 0:128],
                                      waits=[ttr, t_piece_free[sub % 2]],
                                      inc=True)
                            bank_free[bi] = tev
                            t_vc[(j, h)] = p.dma(
                                "gpsimd",
                                dt["vc_d"][j, :, h * 128:(h + 1) * 128],
                                bfs[:, 512 + (sub % 2) * 128:
                                     512 + (sub % 2) * 128 + 128],
                                waits=[tev], sem="dkm")
                            t_piece_free[sub % 2] = t_vc[(j, h)]
                            t_last = ttr
                        t_bfs_free = t_last
        t_xfm_read = tmm4[3]
    t_bk_free = slot_free[0]
    t_bv_free = slot_free[1]

    t_bk_free, t_bv_free = None, None
    t_fsc_dma = None
    tkv_last = None
    for j in range(NT):
        tlk = p.dma("sync", bk[:, :], dt["km_d"][j, :, :],
                    waits=[t_km[(j, NH - 1)]]
                    + ([t_bk_free] if t_bk_free else []), sem="dbk")
        tlv = p.dma("sync", bv[:, :], dt["vc_d"][j, :, :],
                    waits=[t_vc[(j, NH - 1)]]
                    + ([t_bv_free] if t_bv_free else []), sem="dbv")
        tev = None
        for h in range(NH):
            bi = h % 2
            tmm = p.mm(pp[bi][:, 0:128], bk[:, h * 128:(h + 1) * 128],
                       bv[:, h * 128:(h + 1) * 128], True, True,
                       waits=[tlk, tlv] + bw(bi), inc=True)
            tev = p.v("tensor_copy", bfs[:, h * 128:(h + 1) * 128],
                      pp[bi][:, 0:128],
                      waits=[tmm]
                      + ([t_fsc_dma] if (h == 0 and t_fsc_dma) else [])
                      + ([t_piece_free[0], t_piece_free[1], t_bfs_free]
                         if (h == 0 and t_bfs_free) else []),
                      inc=True)
            bank_free[bi] = tev
            t_bk_free = tmm
            t_bv_free = tmm
        t_bfs_free = None
        t_fsc_dma = p.dma("gpsimd",
                          dt["agi_kv0"][j * NH * 128:(j + 1) * NH * 128, :]
                          .rearrange("(h p) f -> p h f", p=128),
                          bfs[:, :].rearrange("p (h f) -> p h f", f=128),
                          waits=[tev], sem="dag")
        tkv_last = t_fsc_dma
    p.wait("gpsimd", [tkv_last])
    cc_kv0 = p.emit("gpsimd", "collective_compute", "AllGather", AL.bypass,
                    replica_groups=[[0, 1, 2, 3], [4, 5, 6, 7]],
                    ins=[dt["agi_kv0"].opt()], outs=[dt["ago_kv0"].opt()],
                    inc="cc")

    SFULL = scS[:, 0:2048]
    SWORK = scS[:, 2048:4096]
    SCIN = scS[:, 4096:6144]
    SWH = lambda h: scS[:, 2048 + h * 128:2048 + (h + 1) * 128]

    def kv0_ap(rank, slot):
        r0 = (rank * NT * NH + slot * NH) * 128
        return (dt["ago_kv0"][r0:r0 + NH * 128, :]
                .rearrange("(h p) f -> p h f", p=128))

    t_sf = p.v("memset", SFULL, 0.0, waits=[t_fsc_dma], inc=True)
    t_rms_sd = None
    t_rms_in = None
    t_scin_free = None
    t_swork_tread = None
    t_ofm_w = []
    t_ofm_ev_last = None
    for j in range(NT):
        if j > 0:
            for rk in range(4):
                tld = p.dma("scalar",
                            SCIN.rearrange("p (h f) -> p h f", f=128),
                            kv0_ap(rk, j - 1),
                            waits=[cc_kv0]
                            + ([t_scin_free] if t_scin_free else []),
                            sem="dsc")
                t_sf = p.v("tensor_tensor", SFULL, SFULL, SCIN, op=AL.add,
                           waits=[tld, t_sf], inc=True)
                t_scin_free = t_sf
        t_sw = p.v("tensor_copy", SWORK, SFULL, waits=[t_sf]
                   + ([t_swork_tread] if t_swork_tread else []), inc=True)
        for rk in range(3):
            tld = p.dma("scalar",
                        SCIN.rearrange("p (h f) -> p h f", f=128),
                        kv0_ap(rk, j),
                        waits=[cc_kv0] + ([t_scin_free] if t_scin_free else []),
                        sem="dsc")
            t_sw = p.v("scalar_tensor_tensor", SWORK, SCIN, PFLAG(rk), SWORK,
                       op0=AL.mult, op1=AL.add, waits=[tld, t_sw], inc=True)
            t_scin_free = t_sw
        tlk = p.dma("sync", bk[:, :], dt["km_d"][j, :, :],
                    waits=[t_bk_free] if t_bk_free else [], sem="dbk")
        tlv = p.dma("sync", bv[:, :], dt["vc_d"][j, :, :],
                    waits=[t_bv_free] if t_bv_free else [], sem="dbv")
        t_kf_tread = None
        for h in range(NH):
            ttr = p.tr(ppb[0][:, 0:128], bk[:, h * 128:(h + 1) * 128], IDENT,
                       waits=[tlk] + bw("b0"), inc=True)
            tkf = p.v("tensor_copy", bfs[:, 0:128], ppb[0][:, 0:128],
                      waits=[ttr, t_fsc_dma]
                      + ([t_kf_tread] if t_kf_tread else []),
                      inc=True)
            bank_free["b0"] = tkf
            ti2s = []
            t_after_kv = None
            for c2 in range(2):
                pb0 = c2 * 64
                qm_ap = bufA[:, h * 1024 + j * 128 + pb0:
                             h * 1024 + j * 128 + pb0 + 64]
                tst_ = p.mm(pp[1][pb0:pb0 + 64, 0:64], bfs[:, pb0:pb0 + 64],
                            qm_ap, True, True,
                            waits=[tkf, tA_q[(h, j // 2)]]
                            + (bw(1) if c2 == 0 else []),
                            inc=True)
                tmask = p.v("tensor_tensor", smlb[pb0:pb0 + 64, 0:64],
                            pp[1][pb0:pb0 + 64, 0:64], TRI64, op=AL.mult,
                            waits=[tst_, tconst], inc=True)
                bank_free[1] = tmask
                ti1 = p.mm(pp[2][pb0:pb0 + 64, 0:128],
                           smlb[pb0:pb0 + 64, 0:64],
                           bv[pb0:pb0 + 64, h * 128:(h + 1) * 128],
                           start=True, stop=False,
                           waits=[tmask, tlv] + (bw(2) if c2 == 0 else []))
                w_in = [t_sw] if c2 == 0 else [t_after_kv]
                ti2 = p.mm(pp[2][pb0:pb0 + 64, 0:128], qm_ap, SWH(h),
                           start=False, stop=True, waits=w_in, inc=True)
                ti2s.append(ti2)
                if c2 == 0:
                    tkv = p.mm(pp[3][:, 0:128],
                               bk[0:64, h * 128:(h + 1) * 128],
                               bv[0:64, h * 128:(h + 1) * 128], True, True,
                               waits=[tlk, tlv] + bw(3), inc=True)
                    t_after_kv = p.v("tensor_tensor", SWH(h), SWH(h),
                                     pp[3][:, 0:128], op=AL.add,
                                     waits=[tkv, ti2, t_sw], inc=True)
                    bank_free[3] = t_after_kv
            t_swork_tread = ti2s[-1]
            t_kf_tread = tst_
            tsq = p.s("activation", bfs[:, 128:256], pp[2][:, 0:128],
                      AF.Square, accum_out=sml[:, 0:1],
                      waits=ti2s + [t_fsc_dma], inc=True)
            tva = p.v("tensor_scalar", sml[:, 1:2], sml[:, 0:1],
                      1.0 / 128.0, 1e-5, op0=AL.mult, op1=AL.add,
                      waits=[tsq, t_rms_sd], inc=True)
            tsd = p.s("activation", sml[:, 2:3], sml[:, 1:2], AF.Sqrt,
                      waits=[tva, t_rms_in], inc=True)
            t_rms_sd = tsd
            tin = p.v("reciprocal", sml[:, 3:4], sml[:, 2:3], waits=[tsd],
                      inc=True)
            t_rms_in = tin
            ton = p.v("tensor_scalar", smlb[:, 128:256], pp[2][:, 0:128],
                      sml[:, 3:4], None, op0=AL.mult,
                      waits=[tin] + bw("b1"), inc=True)
            bank_free[2] = ton
            ttr2 = p.tr(ppb[1][:, 0:128], smlb[:, 128:256], IDENT,
                        waits=[ton] + bw("b1"), inc=True)
            tev = p.v("tensor_copy", xfmh[:, h * 128:(h + 1) * 128],
                      ppb[1][:, 0:128],
                      waits=[ttr2]
                      + ([t_ofm_w[-1]] if (h == 0 and t_ofm_w) else []),
                      inc=True)
            bank_free["b1"] = tev
            t_ofm_ev_last = tev
        t_bk_free = tst_
        t_bv_free = ti2s[-1]
        t_ofm_w.append(p.dma("gpsimd",
                             dt["ofm_d"][j * 16:(j + 1) * 16, :, :]
                             .rearrange("b p f -> p b f"),
                             xfmh[:, 0:2048].rearrange("p (b f) -> p b f",
                                                       f=128),
                             waits=[t_ofm_ev_last], sem="dof"))

    ab_free = [None, None]
    res_tok = {}
    for og in range(4):
        for jb in range(2):
            for h in range(NH):
                wi = h % 2
                twl = p.dma("scalar", wb[wi][:, 0:512],
                            dt["ago_att0"][att0_blk(3, h, og), :, :],
                            waits=([wb_free[wi]] if wb_free[wi] else []),
                            sem="dwb%d" % wi)
                tmmlast = None
                for j4 in range(4):
                    j = jb * 4 + j4
                    ai = (h * 4 + j4) % 2
                    tal = p.dma("sync", ab[ai][:, 0:128],
                                dt["ofm_d"][j * 16 + h, :, :],
                                waits=[t_ofm_w[j]]
                                + ([ab_free[ai]] if ab_free[ai] else []),
                                sem="dab%d" % ai)
                    tmm = p.mm(pp[j4][:, 0:512], ab[ai][:, 0:128],
                               wb[wi][:, 0:512],
                               start=(h == 0), stop=(h == NH - 1),
                               waits=[twl, tal] + (bw(j4) if h == 0 else []),
                               inc=True)
                    ab_free[ai] = tmm
                    tmmlast = tmm
                    if h == NH - 1:
                        tres = p.v("tensor_tensor",
                                   strm[:, j * D + og * 512:j * D + og * 512 + 512],
                                   strm[:, j * D + og * 512:j * D + og * 512 + 512],
                                   pp[j4][:, 0:512], op=AL.add, waits=[tmm],
                                   inc=True)
                        res_tok[(og, j)] = tres
                        bank_free[j4] = tres
                wb_free[wi] = tmmlast
    t_block = [res_tok[(3, j)] for j in range(NT)]

    def layernorm_pass(t_ready_per_tile, make_fm):
        toks = []
        t_bfs_l = [None]
        tsd = None
        for j in range(NT):
            xt = strm[:, j * D:(j + 1) * D]
            tb = None
            for q4 in range(4):
                tb = p.v("bn_stats", sml[:, 8 + q4 * 6:14 + q4 * 6],
                         xt[:, q4 * 512:(q4 + 1) * 512],
                         waits=[t_ready_per_tile[j]], inc=True)
            ta = p.v("bn_aggr", sml[:, 32:34], sml[:, 8:32],
                     waits=[tb, tsd], inc=True)
            tsd = p.s("activation", sml[:, 34:35], sml[:, 33:34], AF.Sqrt,
                      bias=1e-5, waits=[ta], inc=True)
            trc = p.v("reciprocal", sml[:, 35:36], sml[:, 34:35], waits=[tsd],
                      inc=True)
            tap = p.v("tensor_scalar", xt, xt, sml[:, 32:33], sml[:, 35:36],
                      op0=AL.subtract, op1=AL.mult, waits=[trc], inc=True)
            if make_fm:
                tcast = p.s("copy", bfs[:, :], xt,
                            waits=[tap] + ([t_bfs_l[0]] if t_bfs_l[0] else []),
                            inc=True)
                for ch in range(16):
                    bi = "b%d" % (ch % 2)
                    ttr = p.tr(ppb[ch % 2][:, 0:128],
                               bfs[:, ch * 128:(ch + 1) * 128], IDENT,
                               waits=[tcast] + bw(bi), inc=True)
                    tev = p.v("tensor_copy",
                              bufA[:, ch * 1024 + j * 128:
                                   ch * 1024 + j * 128 + 128],
                              ppb[ch % 2][:, 0:128], waits=[ttr], inc=True)
                    bank_free[bi] = tev
                t_bfs_l[0] = ttr
                toks.append(tev)
            else:
                toks.append(tap)
        return toks

    if stop_after == "dbg0":
        tdbg = p.dma("gpsimd", dbg["qm"][:, :], bufA[:, :],
                     waits=[tA_q[(15, 3)]], sem="ddbg")
        tdbg = p.dma("gpsimd", dbg["km"][:, :], dt["km_d"][0, :, :],
                     waits=[t_km[(0, 15)]], sem="ddbg")
        tdbg = p.dma("gpsimd", dbg["vc"][:, :], dt["vc_d"][0, :, :],
                     waits=[t_vc[(0, 15)]], sem="ddbg")
        tdbg = p.dma("gpsimd", dbg["kv"][:, :],
                     dt["agi_kv0"][0:NH * 128, :], waits=[tkv_last], sem="ddbg")
        tdbg = p.dma("gpsimd", dbg["ofm"][:, :],
                     dt["ofm_d"][0:16, :, :].rearrange("b p f -> p b f"),
                     waits=[t_ofm_w[0]], sem="ddbg")
        for j in range(NT):
            tdbg = p.dma("gpsimd", dbg["pre"][j * 128:(j + 1) * 128, :],
                         strm[:, j * D:(j + 1) * D],
                         waits=[res_tok[(3, j)]], sem="ddbg")
        p.wait("gpsimd", [tdbg])
        t_block = [tdbg] * NT

    def finish(tok_list):
        tout = None
        for j in range(NT):
            tout = p.dma("gpsimd", out_ap[j * 128:(j + 1) * 128, :],
                         strm[:, j * D:(j + 1) * D],
                         waits=[tok_list[min(j, len(tok_list) - 1)]],
                         sem="dout")
        p.wait("gpsimd", [tout])

    ln1 = layernorm_pass(t_block,
                         make_fm=(stop_after not in ("l0attn", "dbg0")))

    if stop_after in ("l0attn", "dbg0"):
        finish(ln1)
        build_blocks(nc, p)
        es.close()
        return nc

    t_bk_prep = t_bk_free
    t_bv_prep = t_bv_free
    t_fsc_free = None
    ag2_last = None

    def prep_ternary(src_name, nkc, ncols, agi_name, oc_per_kc):
        nonlocal t_bk_prep, t_bv_prep, t_fsc_free, t_ag_guard
        t_loc = {"fsc": t_fsc_free, "bk": None}
        tw_last = t_ag_guard
        nns = ncols // 512
        for kc in range(nkc):
            for ot in range(ncols // 2048):
                tcast = None
                tabs = None
                for q4 in range(4):
                    w_f = []
                    if t_loc["fsc"]:
                        w_f.append(t_loc["fsc"])
                    tl = p.dma("scalar", fsc[:, :],
                               dt[src_name][kc * 128:(kc + 1) * 128,
                                            ot * 2048 + q4 * 512:
                                            ot * 2048 + (q4 + 1) * 512],
                               waits=w_f, sem="dk1")
                    sl = slice(q4 * 512, (q4 + 1) * 512)
                    wb_ = ([tw_last] if (tw_last and q4 == 0) else [])
                    if t_bk_prep is not None and q4 == 0:
                        wb_ = wb_ + [t_bk_prep]
                    tcast = p.s("copy", bfs[:, sl], fsc[:, :],
                                waits=[tl] + wb_, inc=True)
                    tabs = p.s("activation", bk[:, sl], fsc[:, :], AF.Abs,
                               waits=[tl] + ([tw_last] if tw_last else [])
                               + ([t_bk_prep] if t_bk_prep else []),
                               inc=True)
                    t_bk_prep = None
                    t_loc["fsc"] = tabs
                    t_fsc_free = tabs
                tsum = {}
                for ns in range(4):
                    tsum[ns] = p.mm(
                        pp[ns][0:2, 0:512], cstb[:, 834:836],
                        bk[:, ns * 512:(ns + 1) * 512],
                        True, True, waits=[tabs] + bw(ns), inc=True)
                tssc = {}
                for ns in range(4):
                    tssc[ns] = p.v("tensor_scalar", bv[0:2, ns * 512:(ns + 1) * 512],
                                   pp[ns][0:2, 0:512], 1.0 / 64.0, 1e-8,
                                   op0=AL.mult, op1=AL.max,
                                   waits=[tsum[ns]]
                                   + ([t_bv_prep] if (ns == 0 and t_bv_prep) else []),
                                   inc=True)
                    bank_free[ns] = tssc[ns]
                t_bv_prep = None
                for ns in range(4):
                    sl = slice(ns * 512, (ns + 1) * 512)
                    pb_ = 4 + ns % 2
                    tbc = p.mm(pp[pb_][:, 0:512], c2r[:, 0:128],
                               bv[0:2, ns * 512:(ns + 1) * 512], True, True,
                               waits=[tssc[ns]] + bw(pb_), inc=True)
                    trc = p.v("reciprocal", fsc[:, :], pp[pb_][:, 0:512],
                              waits=[tbc, t_loc["fsc"]], inc=True)
                    tu = p.v("tensor_tensor", bk[:, sl], bfs[:, sl],
                             fsc[:, :], op=AL.mult,
                             waits=[trc, tcast, tabs], inc=True)
                    tcl = p.v("tensor_scalar", fsc[:, :], bk[:, sl],
                              1.0, -1.0, op0=AL.min, op1=AL.max,
                              waits=[tu], inc=True)
                    trd = p.v("tensor_scalar", fsc[:, :], fsc[:, :],
                              MAGIC, None, op0=AL.add,
                              waits=[tcl], inc=True)
                    trd = p.v("tensor_scalar", fsc[:, :], fsc[:, :],
                              MAGIC, None, op0=AL.subtract,
                              waits=[trd], inc=True)
                    twh = p.v("tensor_tensor", bk[:, sl], fsc[:, :],
                              pp[pb_][:, 0:512], op=AL.mult,
                              waits=[trd, tu], inc=True)
                    t_loc["fsc"] = twh
                    t_fsc_free = twh
                    bank_free[pb_] = twh
                    td = p.v("tensor_tensor", bk[:, sl], bk[:, sl],
                             bfs[:, sl], op=AL.subtract, waits=[twh],
                             inc=True)
                    twe = p.v("tensor_tensor", bk[:, sl], bk[:, sl],
                              bfs[:, sl], op=AL.add, waits=[td], inc=True)
                    nk_ = 2 if oc_per_kc == 128 else 8
                    blk0 = (ot * 4 + ns) * nk_ + kc
                    tw_last = p.dma("gpsimd",
                                    dt[agi_name][blk0, :, :],
                                    bk[:, sl], waits=[twe], sem="dag")
        t_ag_guard = tw_last
        return tw_last

    t_ag_guard = None
    t_bk_prep = ln1[-1]
    tw_gu = prep_ternary("gu_s", 2, GU_O, "agi_gu", 128)
    p.wait("gpsimd", [tw_gu])
    cc_gu = p.emit("gpsimd", "collective_compute", "AllGather", AL.bypass,
                   replica_groups=[list(range(8))],
                   ins=[dt["agi_gu"].opt()], outs=[dt["ago_gu"].opt()],
                   inc="cc")
    tw_pr = prep_ternary("proj_s", 8, D, "agi_proj", 16)
    p.wait("gpsimd", [tw_pr])
    cc_pr = p.emit("gpsimd", "collective_compute", "AllGather", AL.bypass,
                   replica_groups=[list(range(8))],
                   ins=[dt["agi_proj"].opt()], outs=[dt["ago_proj"].opt()],
                   inc="cc")

    def gu_blk(k, oc):
        return (k // 2) * 256 + (k % 2) * 128 + oc

    def proj_blk(k, oc):
        return (k // 8) * 128 + (k % 8) * 16 + oc

    def ffn_l0(x_fm_toks):
        res2 = {}
        wbf = [None, None]
        t_bk_ffn = t_ag_guard
        t_bv_ffn = t_ag_guard
        t_h_free = tfq
        t_bfs_f = ("vt", p.cnt["vt"])
        t_fsc_f = ("vt", p.cnt["vt"])
        SLOT_SEM = ["dbk", "dbv", "dk2", "dk3"]
        slot_free = [t_bk_ffn, t_bv_ffn, t_bk_ffn, t_bv_ffn]
        for half in range(2):
            tstat = None
            t_invb = None
            hw = [x_fm_toks[-1]] if half == 0 else [x_fm_toks[-1]]
            t_h = {}
            t_up_stage = t_bfs_f
            for g in range(16):
                tupst = {}
                for gu in (1, 0):
                    ocg = g + (16 if gu == 1 else 0)
                    tmm4 = [None] * 4
                    for kch in range(16):
                        sl_ = kch % 4
                        blk = (kch // 2) * 64 + ocg * 2 + (kch % 2)
                        twl = p.dma(["sync", "scalar"][kch % 2],
                                    bk[:, sl_ * 512:(sl_ + 1) * 512],
                                    dt["ago_gu"][blk, :, :],
                                    waits=([slot_free[sl_]] if slot_free[sl_]
                                           else [(SLOT_SEM[sl_], 0)])
                                    + ([cc_gu] if (half == 0 and g == 0
                                                   and gu == 1 and kch < 4)
                                       else []),
                                    sem=SLOT_SEM[sl_])
                        for c in range(4):
                            tmm4[c] = p.mm(
                                pp[c][:, 0:512],
                                bk[:, sl_ * 512 + c * 128:sl_ * 512 + c * 128 + 128],
                                bufA[:, kch * 1024 + half * 512:
                                     kch * 1024 + half * 512 + 512],
                                start=(kch == 0), stop=(kch == 15),
                                waits=[twl] + hw + (bw(c) if kch == 0 else []),
                                inc=(c == 3))
                        slot_free[sl_] = tmm4[3]
                    if gu == 1:
                        for c in range(4):
                            tupst[c] = p.v("tensor_copy",
                                           bfs[:, c * 512:(c + 1) * 512],
                                           pp[c][:, 0:512],
                                           waits=[tmm4[3], t_up_stage],
                                           inc=True)
                            bank_free[c] = tupst[c]
                for c in range(4):
                    i = g * 4 + c
                    tsg = p.s("activation", fsc[:, :], pp[c][:, 0:512],
                              AF.Silu, waits=[tmm4[3]]
                              + ([t_fsc_f] if t_fsc_f else []), inc=True)
                    t_fsc_f = tsg
                    hsl = slice(i * 512, (i + 1) * 512)
                    th = p.v("tensor_tensor", bufH[:, hsl], fsc[:, :],
                             bfs[:, c * 512:(c + 1) * 512], op=AL.mult,
                             waits=[tsg, tupst[c]]
                             + ([t_h_free] if (i == 0 and t_h_free) else []),
                             inc=True)
                    t_h[i] = th
                    bank_free[c] = th
                    th2 = p.s("activation", smlb[:, 0:128] if False else
                              bfs[:, c * 512:(c + 1) * 512], bufH[:, hsl],
                              AF.Square, waits=[th], inc=True)
                    tstat = p.mm(pp[4][0:1, 0:512],
                                 ONES128, bfs[:, c * 512:(c + 1) * 512],
                                 start=(i == 0), stop=(i == 63),
                                 waits=[th2] + ([t_invb] if (i == 0 and t_invb)
                                                else []),
                                 inc=(i == 63))
                t_up_stage = tstat
            t_h_free = None
            t_bk_ffn = slot_free[0]
            t_bv_ffn = slot_free[1]
            t_bfs_f = tstat
            tms = p.v("tensor_scalar", fsc[0:1, 0:512], pp[4][0:1, 0:512],
                      1.0 / 8192.0, F32EPS, op0=AL.mult, op1=AL.add,
                      waits=[tstat, t_fsc_f if t_fsc_f else ("vt", 0)],
                      inc=True)
            bank_free[4] = tms
            tsq2 = p.s("activation", pp[5][0:1, 0:512], fsc[0:1, 0:512],
                       AF.Sqrt, waits=[tms] + bw(5), inc=True)
            tinv = p.v("reciprocal", fsc[0:1, 0:512], pp[5][0:1, 0:512],
                       waits=[tsq2], inc=True)
            t_invb = p.mm(pp[5][:, 0:512], ONESROW, fsc[0:1, 0:512],
                          True, True, waits=[tinv], inc=True)
            t_fsc_f = t_invb
            t_hn = {}
            for i in range(64):
                hsl = slice(i * 512, (i + 1) * 512)
                t_hn[i] = p.v("tensor_tensor", bufH[:, hsl], bufH[:, hsl],
                              pp[5][:, 0:512], op=AL.mult,
                              waits=[t_invb, t_h[i]], inc=True)
            bank_free[5] = t_hn[63]
            for og2 in range(4):
                tmm4 = [None] * 4
                for kch in range(64):
                    sl_ = kch % 4
                    blk = (kch // 8) * 32 + og2 * 8 + (kch % 8)
                    twl = p.dma(["sync", "scalar"][kch % 2],
                                bk[:, sl_ * 512:(sl_ + 1) * 512],
                                dt["ago_proj"][blk, :, :],
                                waits=([slot_free[sl_]] if slot_free[sl_]
                                       else [(SLOT_SEM[sl_], 0)])
                                + ([cc_pr] if (half == 0 and og2 == 0
                                               and kch < 4) else []),
                                sem=SLOT_SEM[sl_])
                    for c in range(4):
                        tmm4[c] = p.mm(
                            pp[c][:, 0:512],
                            bk[:, sl_ * 512 + c * 128:sl_ * 512 + c * 128 + 128],
                            bufH[:, kch * 512:(kch + 1) * 512],
                            start=(kch == 0), stop=(kch == 63),
                            waits=[twl, t_hn[kch]] + (bw(c) if kch == 0 else []),
                            inc=(c == 3))
                    slot_free[sl_] = tmm4[3]
                for c in range(4):
                    ot = og2 * 4 + c
                    tev = p.s("copy", bfs[:, 0:512], pp[c][:, 0:512],
                              waits=[tmm4[3], t_bfs_f], inc=True)
                    bank_free[c] = tev
                    for sub in range(4):
                        j = half * 4 + sub
                        bi = "b%d" % (sub % 2)
                        ttr = p.tr(ppb[sub % 2][:, 0:128],
                                   bfs[:, sub * 128:(sub + 1) * 128], IDENT,
                                   waits=[tev] + bw(bi), inc=True)
                        tres = p.v("tensor_tensor",
                                   strm[:, j * D + ot * 128:j * D + ot * 128 + 128],
                                   strm[:, j * D + ot * 128:j * D + ot * 128 + 128],
                                   ppb[sub % 2][:, 0:128], op=AL.add,
                                   waits=[ttr] + ([res2[(j, ot)]] if (j, ot) in res2
                                                  else []),
                                   inc=True)
                        res2[(j, ot)] = tres
                        bank_free[bi] = tres
                    t_bfs_f = ttr
            t_bk_ffn = slot_free[0]
            t_bv_ffn = slot_free[1]
        return [res2[(j, 15)] for j in range(NT)]

    rtoks = ffn_l0(ln1)
    ln2 = layernorm_pass(rtoks, make_fm=(stop_after != "l0ffn"))

    if stop_after == "l0ffn":
        finish(ln2)
        build_blocks(nc, p)
        es.close()
        return nc

    ag1_last = t_ag_guard
    for m in range(4):
        for kc in range(2):
            tcast = None
            for q4 in range(4):
                tl = p.dma("scalar", fsc[:, :],
                           dt["att1_s"][(m * 2 + kc) * 128:
                                        (m * 2 + kc + 1) * 128,
                                        q4 * 512:(q4 + 1) * 512],
                           waits=[t_fsc_free], sem="dk1")
                tcast = p.v("tensor_copy", bfs[:, q4 * 512:(q4 + 1) * 512],
                            fsc[:, :],
                            waits=[tl] + ([ag1_last] if q4 == 0 else []),
                            inc=True)
                t_fsc_free = tcast
            for ocg in range(4):
                blk0 = (m * 4 + ocg) * 2 + kc
                ag1_last = p.dma("gpsimd", dt["agi_att1"][blk0, :, :],
                                 bfs[:, ocg * 512:(ocg + 1) * 512],
                                 waits=[tcast], sem="dag")
    p.wait("gpsimd", [ag1_last])
    cc_att1 = p.emit("gpsimd", "collective_compute", "AllGather", AL.bypass,
                     replica_groups=[list(range(8))],
                     ins=[dt["agi_att1"].opt()], outs=[dt["ago_att1"].opt()],
                     inc="cc")
    t_ag_guard = ag1_last

    tpart = {}
    t_acc_last = None
    t_prev_acc = None
    for mi, (nm, nkc) in enumerate((("wg_s", 2), ("wu_s", 2), ("wd_s", 8))):
        ncols = FF_H if nkc == 2 else D
        tacc = None
        for kc in range(nkc):
            for ot in range(ncols // 2048):
                for q4 in range(4):
                    tl = p.dma("scalar", fsc[:, :],
                               dt[nm][kc * 128:(kc + 1) * 128,
                                      ot * 2048 + q4 * 512:
                                      ot * 2048 + (q4 + 1) * 512],
                               waits=[t_fsc_free], sem="dk1")
                    trd_ = p.v("reduce_sum", sml[:, 36:37], fsc[:, :],
                               axis=AX.X, apply_absolute_value=True,
                               waits=[tl, tacc if tacc else t_prev_acc],
                               inc=True)
                    t_fsc_free = trd_
                    if tacc is None:
                        tacc = p.v("tensor_copy", sml[:, 37 + mi:38 + mi],
                                   sml[:, 36:37], waits=[trd_], inc=True)
                    else:
                        tacc = p.v("tensor_tensor", sml[:, 37 + mi:38 + mi],
                                   sml[:, 37 + mi:38 + mi], sml[:, 36:37],
                                   op=AL.add, waits=[trd_], inc=True)
        tpart[mi] = p.g("tensor_reduce", sml[0:1, 40 + mi:41 + mi],
                        sml[:, 37 + mi:38 + mi], axis=AX.C, op=AL.add,
                        waits=[tacc], inc=True)
        t_acc_last = tpart[mi]
        t_prev_acc = tacc
    tz4 = p.v("memset", sml[0:1, 43:44], 0.0, waits=[t_acc_last], inc=True)
    tsums = p.dma("gpsimd", dt["agi_sums"][0:1, 0:4], sml[0:1, 40:44],
                  waits=[t_acc_last, tz4], sem="dag")
    p.wait("gpsimd", [tsums])
    cc_sum = p.emit("gpsimd", "collective_compute", "AllReduce", AL.add,
                    replica_groups=[list(range(8))],
                    ins=[dt["agi_sums"].opt()], outs=[dt["ago_sums"].opt()],
                    inc="cc")
    tsl = p.dma("gpsimd", sml[0:1, 44:48], dt["ago_sums"][0:1, 0:4],
                waits=[cc_sum], sem="dcs")
    tmean = {}
    tsw = {}
    for mi in range(3):
        n_el = float(1.0 / 16777216.0)
        tmean[mi] = p.v("tensor_scalar", sml[0:1, 48 + mi:49 + mi],
                        sml[0:1, 44 + mi:45 + mi], 1.0 / 16777216.0, 1e-5,
                        op0=AL.mult, op1=AL.max, waits=[tsl], inc=True)
        tsw[mi] = p.v("reciprocal", sml[0:1, 52 + mi:53 + mi],
                      sml[0:1, 48 + mi:49 + mi], waits=[tmean[mi]], inc=True)
    ISW = lambda mi: cstf[:, 393 + mi:394 + mi]
    SWP = lambda mi: cstf[:, 396 + mi:397 + mi]
    tbc_l = []
    for mi in range(3):
        tm1_ = p.mm(pp[4][:, 0:1], ONESROW, sml[0:1, 48 + mi:49 + mi],
                    True, True, waits=[tmean[mi]] + bw(4), inc=True)
        t1_ = p.v("tensor_copy", ISW(mi), pp[4][:, 0:1], waits=[tm1_],
                  inc=True)
        bank_free[4] = t1_
        tm2_ = p.mm(pp[4][:, 0:1], ONESROW, sml[0:1, 52 + mi:53 + mi],
                    True, True, waits=[tsw[mi]] + bw(4), inc=True)
        t2_ = p.v("tensor_copy", SWP(mi), pp[4][:, 0:1], waits=[tm2_],
                  inc=True)
        bank_free[4] = t2_
        tbc_l += [t1_, t2_]
    t_swbc = tbc_l[-1]

    def wg_blk(k, oc):
        return (k // 2) * 128 + (k % 2) * 64 + oc

    def wd_blk(k, oc):
        return (k // 8) * 128 + (k % 8) * 16 + oc

    for mi, (nm, nkc, agn, opk) in enumerate(
            (("wg_s", 2, "agi_wg", 64), ("wu_s", 2, "agi_wu", 64),
             ("wd_s", 8, "agi_wd", 16))):
        smi = mi if mi < 2 else 2
        ncols = FF_H if nkc == 2 else D
        for kc in range(nkc):
            for ot in range(ncols // 2048):
                for q4 in range(4):
                    tl = p.dma("scalar", fsc[:, :],
                               dt[nm][kc * 128:(kc + 1) * 128,
                                      ot * 2048 + q4 * 512:
                                      ot * 2048 + (q4 + 1) * 512],
                               waits=[t_fsc_free], sem="dk1")
                    tv1_ = p.v("tensor_scalar", fsc[:, :], fsc[:, :],
                               SWP(smi), MAGIC, op0=AL.mult, op1=AL.add,
                               waits=[tl, t_swbc], inc=True)
                    tv2_ = p.v("tensor_scalar", fsc[:, :], fsc[:, :],
                               MAGIC, 1.0, op0=AL.subtract, op1=AL.min,
                               waits=[tv1_], inc=True)
                    tv3_ = p.v("tensor_scalar",
                               bfs[:, q4 * 512:(q4 + 1) * 512], fsc[:, :],
                               -1.0, None, op0=AL.max,
                               waits=[tv2_, t_ag_guard if q4 == 0 else None],
                               inc=True)
                    t_fsc_free = tv3_
                for ns4 in range(4):
                    blk0 = (ot * 4 + ns4) * nkc + kc
                    t_ag_guard = p.dma("gpsimd", dt[agn][blk0, :, :],
                                       bfs[:, ns4 * 512:(ns4 + 1) * 512],
                                       waits=[tv3_], sem="dag")
        p.wait("gpsimd", [t_ag_guard])
        p.emit("gpsimd", "collective_compute", "AllGather", AL.bypass,
               replica_groups=[list(range(8))],
               ins=[dt[agn].opt()], outs=[dt["ago_" + agn[4:]].opt()],
               inc="cc")
    cc_wd = ("cc", p.cnt["cc"])

    def att1_blk(m, kch, ocg):
        return (kch // 2) * 32 + (m * 4 + ocg) * 2 + (kch % 2)

    wbf = [None, None]
    t_bfs_f = t_ag_guard
    tq_piece = {}
    tk_piece = {}
    tv_piece = {}
    p.wait("tensor", [cc_att1])
    x2rd = ln2[-1]
    for m in range(3):
        for oc in range(NH):
            pb_ = 4 + (oc % 2)
            tmmh = {}
            for kch in range(16):
                wi = kch % 2
                twl = p.dma("sync", wb[wi][:, 0:128],
                            dt["ago_att1"][att1_blk(m, kch, oc // 4), :,
                                           (oc % 4) * 128:(oc % 4) * 128 + 128],
                            waits=([wbf[wi]] if wbf[wi] else [])
                            + ([cc_att1] if (m == 0 and oc == 0 and kch < 2)
                               else []),
                            sem="dwb%d" % wi)
                for hf in range(2):
                    tmmh[hf] = p.mm(pp[pb_ - 4 + hf * 2][:, 0:512],
                                    wb[wi][:, 0:128],
                                    bufA[:, kch * 1024 + hf * 512:
                                         kch * 1024 + (hf + 1) * 512],
                                    start=(kch == 0), stop=(kch == 15),
                                    waits=[twl, x2rd]
                                    + (bw(pb_ - 4 + hf * 2) if kch == 0 else []),
                                    inc=True)
                wbf[wi] = tmmh[1]
            for hf in range(2):
                bnk = pb_ - 4 + hf * 2
                te = p.v("tensor_scalar", bfs[:, 0:512], pp[bnk][:, 0:512],
                         sb["l1b"][:, m * 16 + oc:m * 16 + oc + 1], None,
                         op0=AL.add, waits=[tmmh[hf], t_bfs_f], inc=True)
                bank_free[bnk] = te
                for sub in range(4):
                    j = hf * 4 + sub
                    bi = "b%d" % (sub % 2)
                    ttr = p.tr(ppb[sub % 2][:, 0:128],
                               bfs[:, sub * 128:(sub + 1) * 128], IDENT,
                               waits=[te] + bw(bi), inc=True)
                    tevp = p.v("tensor_copy",
                               bfs[:, 512 + sub * 128:512 + (sub + 1) * 128],
                               ppb[sub % 2][:, 0:128],
                               waits=[ttr, t_piece_free[sub % 2]], inc=True)
                    bank_free[bi] = tevp
                    if m == 0:
                        tp = p.dma("gpsimd",
                                   dt["qtm_d"][j, :, oc * 128:(oc + 1) * 128],
                                   bfs[:, 512 + sub * 128:512 + (sub + 1) * 128],
                                   waits=[tevp], sem="dkm")
                        tq_piece[(j, oc)] = tp
                    elif m == 1:
                        tp = p.dma("gpsimd",
                                   dt["ktm_d"][j, :, oc * 128:(oc + 1) * 128],
                                   bfs[:, 512 + sub * 128:512 + (sub + 1) * 128],
                                   waits=[tevp], sem="dkm")
                        tk_piece[(j, oc)] = tp
                    else:
                        tp = p.dma("gpsimd",
                                   dt["agi_kv1"][128 + oc * 8 + j, :, :],
                                   bfs[:, 512 + sub * 128:512 + (sub + 1) * 128],
                                   waits=[tevp], sem="dkm")
                        tv_piece[(j, oc)] = tp
                    t_piece_free[sub % 2] = tp
                t_bfs_f = ttr

    COS_ = lambda: bv[:, :]
    SIN_ = lambda: scS[:, 0:2048]
    T1_ = lambda: scS[:, 2048:4096]
    T2_ = lambda: scS[:, 4096:6144]
    t_scs_free = ("vt", p.cnt["vt"])
    t_bk_f2 = ("tt", p.cnt["tt"])
    t_bv_f2 = ("vt", p.cnt["vt"])
    tq_rope = {}
    tk_stage = {}
    for j in range(NT):
        tcl_ = p.dma("sync", bv[:, :], dt["cos_t"][j * 128:(j + 1) * 128, :],
                     waits=[t_bv_f2] + [("dbk", 0), ("dbv", 0), ("dk2", 0), ("dk3", 0)], sem="dbv")
        tsl_ = p.dma("sync", SIN_(), dt["sin_t"][j * 128:(j + 1) * 128, :],
                     waits=[t_scs_free], sem="dsc")
        for qk in range(2):
            dsrc = "qtm_d" if qk == 0 else "ktm_d"
            tld = p.dma("sync", bk[:, :], dt[dsrc][j, :, :],
                        waits=[t_bk_f2,
                               tq_piece[(j, 15)] if qk == 0 else tk_piece[(j, 15)]]
                        + [("dbk", 0), ("dbv", 0), ("dk2", 0), ("dk3", 0)], sem="dbk")
            tt1 = p.v("tensor_tensor", T1_(), bk[:, :], COS_(), op=AL.mult,
                      waits=[tld, tcl_, t_scs_free], inc=True)
            ke = bk[:, :].rearrange("p (a two) -> p a two", two=2)
            se = SIN_().rearrange("p (a two) -> p a two", two=2)
            t2e = T2_().rearrange("p (a two) -> p a two", two=2)
            tt2a = p.v("tensor_tensor", t2e[:, :, 0], ke[:, :, 1],
                       se[:, :, 0], op=AL.mult, waits=[tld, tsl_], inc=True)
            tt2b = p.v("tensor_tensor", t2e[:, :, 1], ke[:, :, 0],
                       se[:, :, 1], op=AL.mult, waits=[tld, tsl_], inc=True)
            t1e = T1_().rearrange("p (a two) -> p a two", two=2)
            oute = bfs[:, :].rearrange("p (a two) -> p a two", two=2)
            to1 = p.v("tensor_tensor", oute[:, :, 0], t1e[:, :, 0],
                      t2e[:, :, 0], op=AL.subtract,
                      waits=[tt1, tt2a, t_bfs_f], inc=True)
            to2 = p.v("tensor_tensor", oute[:, :, 1], t1e[:, :, 1],
                      t2e[:, :, 1], op=AL.add, waits=[tt1, tt2b], inc=True)
            t_scs_free = to2
            t_bk_f2 = to2
            tevl = None
            for ch in range(16):
                bi = "b%d" % (ch % 2)
                ttr = p.tr(ppb[ch % 2][:, 0:128],
                           bfs[:, ch * 128:(ch + 1) * 128], IDENT,
                           waits=[to1, to2] + bw(bi), inc=True)
                if qk == 0:
                    tev = p.v("tensor_copy",
                              bufA[:, ch * 1024 + j * 128:
                                   ch * 1024 + j * 128 + 128],
                              ppb[ch % 2][:, 0:128], waits=[ttr], inc=True)
                    tq_rope[j] = tev
                else:
                    tev = p.v("tensor_copy",
                              bk[:, ch * 128:(ch + 1) * 128],
                              ppb[ch % 2][:, 0:128],
                              waits=[ttr] + ([tk_stage[j]] if (ch == 0 and j in tk_stage) else []),
                              inc=True)
                bank_free[bi] = tev
                tevl = ttr
            t_bfs_f = tevl
            if qk == 1:
                tk_stage[j] = p.dma(
                    "gpsimd",
                    dt["agi_kv1"][j:j + 121:8, :, :]
                    .rearrange("b p f -> p b f"),
                    bk[:, :].rearrange("p (ch t) -> p ch t", t=128),
                    waits=[tev], sem="dkm")
                t_bk_f2 = tk_stage[j]
                t_bv_f2 = tt1
    p.wait("gpsimd", [tk_stage[NT - 1], tv_piece[(NT - 1, NH - 1)]])
    cc_kv1 = p.emit("gpsimd", "collective_compute", "AllGather", AL.bypass,
                    replica_groups=[[0, 1, 2, 3], [4, 5, 6, 7]],
                    ins=[dt["agi_kv1"].opt()], outs=[dt["ago_kv1"].opt()],
                    inc="cc")

    MROW = lambda c: sml[:, 36 + c:37 + c]
    t_ofm_w1 = []
    kbuf_free = [None, None]
    vbuf_free = [None, None]
    t_acc_free = None
    t_ofm_ev = None
    for j in range(NT):
        t_ofm_uses = None
        for h in range(NH):
            nslot = j + 1
            t_l = None
            t_m_prev = None
            t_acc = None
            texp_prev = None
            tcr_prev = None
            for s_ in range(nslot):
                kb = wb[s_ % 2]
                vb_ = [bk, bv][s_ % 2]
                kap = dt["ago_kv1"].rearrange("(r b) p f -> r b p f",
                                              b=256)[:, h * 8 + s_]
                tkl = p.dma("sync",
                            kb[:, 0:512].rearrange("p (r t) -> p r t", t=128),
                            kap.rearrange("r p f -> p r f"),
                            waits=[cc_kv1] + ([kbuf_free[s_ % 2]]
                                              if kbuf_free[s_ % 2] else []),
                            sem="dwb%d" % (s_ % 2))
                hb = h // 8
                vap = dt["ago_kv1"].rearrange("(r b) p f -> r b p f",
                                              b=256)[:, 128 + h * 8 + s_]
                tvl = p.dma("scalar",
                            vb_[:, 0:512].rearrange("p (r t) -> p r t", t=128),
                            vap.rearrange("r p f -> p r f"),
                            waits=[cc_kv1] + ([vbuf_free[s_ % 2]]
                                              if vbuf_free[s_ % 2]
                                              else [("dk2", 0), ("dk3", 0)]),
                            sem="db%s" % ("k" if s_ % 2 == 0 else "v"))
                tsc = p.mm(pp[0][:, 0:512],
                           bufA[:, h * 1024 + j * 128:h * 1024 + j * 128 + 128],
                           kb[:, 0:512], True, True,
                           waits=[tkl, tq_rope[NT - 1]] + bw(0), inc=True)
                kbuf_free[s_ % 2] = tsc
                tmx = p.v("reduce_max", sml[:, 50:51], pp[0][:, 0:512],
                          axis=AX.X, waits=[tsc, t_m_prev], inc=True)
                if s_ == 0:
                    tmn = p.v("tensor_copy", MROW(s_ % 2), sml[:, 50:51],
                              waits=[tmx], inc=True)
                else:
                    tmn = p.v("tensor_tensor", MROW(s_ % 2), sml[:, 50:51],
                              MROW(1 - s_ % 2), op=AL.max, waits=[tmx],
                              inc=True)
                tbx = p.v("tensor_scalar", sml[:, 52:53], MROW(s_ % 2),
                          -ATTSCALE, None, op0=AL.mult,
                          waits=[tmn, texp_prev], inc=True)
                texp = p.s("activation", bfs[:, 0:512], pp[0][:, 0:512],
                           AF.Exp, bias=sml[:, 52:53], scale=ATTSCALE,
                           waits=[tbx, texp_prev], inc=True)
                bank_free[0] = texp
                tmask = None
                if s_ == j:
                    texp = p.v("tensor_tensor", bfs[:, 0:512], bfs[:, 0:512],
                               DIAG, op=AL.mult, waits=[texp], inc=True)
                tps = p.v("reduce_sum", sml[:, 53:54], bfs[:, 0:512],
                          axis=AX.X, waits=[texp], inc=True)
                if s_ == 0:
                    t_l = p.v("tensor_copy", sml[:, 54:55], sml[:, 53:54],
                              waits=[tps], inc=True)
                else:
                    tdm = p.v("tensor_tensor", sml[:, 55:56],
                              MROW(1 - s_ % 2), MROW(s_ % 2), op=AL.subtract,
                              waits=[tmn, tcr_prev], inc=True)
                    tcr = p.s("activation", sml[:, 56:57], sml[:, 55:56],
                              AF.Exp, scale=ATTSCALE,
                              waits=[tdm, t_acc, t_l], inc=True)
                    tcr_prev = tcr
                    t_l = p.v("scalar_tensor_tensor", sml[:, 54:55],
                              sml[:, 54:55], sml[:, 56:57], sml[:, 53:54],
                              op0=AL.mult, op1=AL.add, waits=[tps, tcr, t_l],
                              inc=True)
                tpts = []
                for g in range(4):
                    ttr = p.tr(ppb[g % 2][:, 0:128],
                               bfs[:, g * 128:(g + 1) * 128], IDENT,
                               waits=[texp, tps] + bw("b%d" % (g % 2)),
                               inc=True)
                    tevt = p.v("tensor_copy",
                               bfs[:, 512 + g * 128:512 + (g + 1) * 128],
                               ppb[g % 2][:, 0:128],
                               waits=[ttr, t_piece_free[g % 2]], inc=True)
                    t_piece_free[g % 2] = tevt
                    bank_free["b%d" % (g % 2)] = tevt
                    tpts.append(tevt)
                tpv = None
                for g in range(4):
                    tpv = p.mm(pp[1][:, 0:128],
                               bfs[:, 512 + g * 128:512 + (g + 1) * 128],
                               vb_[:, g * 128:(g + 1) * 128],
                               start=(g == 0), stop=(g == 3),
                               waits=[tpts[g], tvl] + (bw(1) if g == 0 else []),
                               inc=True)
                vbuf_free[s_ % 2] = tpv
                if s_ == 0:
                    t_acc = p.v("tensor_copy", fsc[:, 0:128], pp[1][:, 0:128],
                                waits=[tpv, t_acc_free], inc=True)
                else:
                    t_acc = p.v("scalar_tensor_tensor", fsc[:, 0:128],
                                fsc[:, 0:128], sml[:, 56:57], pp[1][:, 0:128],
                                op0=AL.mult, op1=AL.add,
                                waits=[tpv, t_acc], inc=True)
                bank_free[1] = t_acc
                t_m_prev = tmn
                texp_prev = texp
            trl = p.v("reciprocal", sml[:, 57:58], sml[:, 54:55],
                      waits=[t_l], inc=True)
            to_ = p.v("tensor_scalar", bfs[:, 1024:1152], fsc[:, 0:128],
                      sml[:, 57:58], None, op0=AL.mult,
                      waits=[trl, t_acc], inc=True)
            t_acc_free = to_
            ttro = p.tr(ppb[0][:, 0:128], bfs[:, 1024:1152], IDENT,
                        waits=[to_] + bw("b0"), inc=True)
            t_ofm_uses = p.v("tensor_copy", xfmh[:, h * 128:(h + 1) * 128],
                             ppb[0][:, 0:128],
                             waits=[ttro]
                             + ([t_ofm_w1[-1]] if (h == 0 and t_ofm_w1) else []),
                             inc=True)
            bank_free["b0"] = t_ofm_uses
        t_ofm_w1.append(p.dma("gpsimd",
                              dt["ofm_d"][j * 16:(j + 1) * 16, :, :]
                              .rearrange("b p f -> p b f"),
                              xfmh[:, 0:2048].rearrange("p (b f) -> p b f",
                                                        f=128),
                              waits=[t_ofm_uses], sem="dof"))

    res3 = {}
    ab_free = [None, None]
    t_borow = None
    for og in range(4):
        tbr = p.dma("sync", fsc[0:1, 0:512],
                    dt["borow"][0:1, og * 512:(og + 1) * 512],
                    waits=[t_borow if t_borow else t_acc_free], sem="dk4")
        for jb in range(2):
            for h in range(NH):
                wi = h % 2
                twl = p.dma("scalar", wb[wi][:, 0:512],
                            dt["ago_att1"][att1_blk(3, h, og), :, :],
                            waits=([wbf[wi]] if wbf[wi] else []),
                            sem="dwb%d" % wi)
                for j4 in range(4):
                    j = jb * 4 + j4
                    ai = (h * 4 + j4) % 2
                    tal = p.dma("sync", ab[ai][:, 0:128],
                                dt["ofm_d"][j * 16 + h, :, :],
                                waits=[t_ofm_w1[j]]
                                + ([ab_free[ai]] if ab_free[ai] else []),
                                sem="dab%d" % ai)
                    tmm = p.mm(pp[j4][:, 0:512], ab[ai][:, 0:128],
                               wb[wi][:, 0:512],
                               start=(h == 0), stop=False,
                               waits=[twl, tal] + (bw(j4) if h == 0 else []),
                               inc=True)
                    ab_free[ai] = tmm
                    if h == NH - 1:
                        tmm = p.mm(pp[j4][:, 0:512], ONESROW,
                                   fsc[0:1, 0:512], start=False, stop=True,
                                   waits=[tbr], inc=True)
                        tres = p.v("tensor_tensor",
                                   strm[:, j * D + og * 512:j * D + og * 512 + 512],
                                   strm[:, j * D + og * 512:j * D + og * 512 + 512],
                                   pp[j4][:, 0:512], op=AL.add, waits=[tmm],
                                   inc=True)
                        res3[(og, j)] = tres
                        bank_free[j4] = tres
                wbf[wi] = tmm
        t_borow = tmm
    ln3 = layernorm_pass([res3[(3, j)] for j in range(NT)], make_fm=False)

    if stop_after == "l1attn":
        finish(ln3)
        build_blocks(nc, p)
        es.close()
        return nc

    res4 = {}
    t_bfs_f = ("vt", p.cnt["vt"])
    t_fsc_f = ("vt", p.cnt["vt"])
    t_A_f = ("tt", p.cnt["tt"])
    t_bk_ffn = vbuf_free[0]
    t_bv_ffn = vbuf_free[1]
    SLOT_SEM = ["dbk", "dbv", "dk2", "dk3"]
    slot_free = [t_bk_ffn, t_bv_ffn, t_bk_ffn, t_bv_ffn]
    INVS = lambda tile: sml[:, 60 + tile:61 + tile]
    for half in range(2):
        tq_tok = {}
        for sub in range(4):
            j = half * 4 + sub
            xt = strm[:, j * D:(j + 1) * D]
            tmx = p.v("reduce_max", sml[:, 50:51], xt, axis=AX.X,
                      apply_absolute_value=True, waits=[ln3[j]], inc=True)
            tmc = p.v("tensor_scalar", sml[:, 51:52], sml[:, 50:51], 1e-5,
                      None, op0=AL.max, waits=[tmx], inc=True)
            trc = p.v("reciprocal", sml[:, 52:53], sml[:, 51:52],
                      waits=[tmc], inc=True)
            tst = p.v("tensor_scalar", sml[:, 53:54], sml[:, 52:53], 128.0,
                      None, op0=AL.mult, waits=[trc], inc=True)
            tiv = p.v("tensor_scalar", INVS(j), sml[:, 51:52], 1.0 / 128.0,
                      None, op0=AL.mult, waits=[tmc], inc=True)
            for q4 in range(4):
                sl = slice(q4 * 512, (q4 + 1) * 512)
                tv1_ = p.v("tensor_scalar", fsc[:, :], xt[:, sl],
                           sml[:, 53:54], MAGIC, op0=AL.mult, op1=AL.add,
                           waits=[tst, t_fsc_f], inc=True)
                tv2_ = p.v("tensor_scalar", fsc[:, :], fsc[:, :], MAGIC,
                           127.0, op0=AL.subtract, op1=AL.min,
                           waits=[tv1_], inc=True)
                tv3_ = p.v("tensor_scalar", bfs[:, sl], fsc[:, :], -128.0,
                           None, op0=AL.max, waits=[tv2_, t_bfs_f], inc=True)
                t_fsc_f = tv3_
                t_bfs_f = tv3_
            for ch in range(16):
                bi = "b%d" % (ch % 2)
                ttr = p.tr(ppb[ch % 2][:, 0:128],
                           bfs[:, ch * 128:(ch + 1) * 128], IDENT,
                           waits=[tv3_, t_A_f] + bw(bi), inc=True)
                tev = p.v("tensor_copy",
                          bufA[:, ch * 1024 + half * 512 + sub * 128:
                               ch * 1024 + half * 512 + sub * 128 + 128],
                          ppb[ch % 2][:, 0:128], waits=[ttr], inc=True)
                bank_free[bi] = tev
            tq_tok[sub] = tev
            t_bfs_f = ttr
        if stop_after == "l1f_q" and half == 0:
            finish([tq_tok[3]] * NT)
            build_blocks(nc, p)
            es.close()
            return nc
        trow = []
        for sub in range(4):
            j = half * 4 + sub
            ttr = p.tr(pp[4][0:1, 0:128], INVS(j), IDENTF,
                       waits=[tq_tok[sub]] + bw(4), inc=True)
            tcp = p.v("tensor_copy", fsc[0:1, sub * 128:(sub + 1) * 128],
                      pp[4][0:1, 0:128], waits=[ttr, t_fsc_f], inc=True)
            bank_free[4] = tcp
            trow.append(tcp)
        trg = p.v("tensor_scalar", fsc[1:2, 0:512] if False else fsc[0:1, 0:512],
                  fsc[0:1, 0:512], sml[0:1, 48:49], None, op0=AL.mult,
                  waits=trow, inc=True)
        t_rbg = p.mm(pp[4][:, 0:512], ONESROW, fsc[0:1, 0:512], True, True,
                     waits=[trg], inc=True)
        t_fsc_f = t_rbg
        t_h = {}
        t_up_stage = t_bfs_f
        tg2 = None
        for g in range(16):
            tupst = {}
            tmm4 = [None] * 4
            for gu, agn in ((1, "ago_wu"), (0, "ago_wg")):
                for kch in range(16):
                    sl_ = kch % 4
                    blk = (kch // 2) * 32 + g * 2 + (kch % 2)
                    twl = p.dma(["sync", "scalar"][kch % 2],
                                bk[:, sl_ * 512:(sl_ + 1) * 512],
                                dt[agn][blk, :, :],
                                waits=([slot_free[sl_]] if slot_free[sl_]
                                       else [(SLOT_SEM[sl_], 0)])
                                + ([cc_wd] if (half == 0 and g == 0
                                               and gu == 1 and kch < 4)
                                   else []),
                                sem=SLOT_SEM[sl_])
                    for c in range(4):
                        tmm4[c] = p.mm(
                            pp[c][:, 0:512],
                            bk[:, sl_ * 512 + c * 128:sl_ * 512 + c * 128 + 128],
                            bufA[:, kch * 1024 + half * 512:
                                 kch * 1024 + half * 512 + 512],
                            start=(kch == 0), stop=(kch == 15),
                            waits=[twl, tq_tok[3]] + (bw(c) if kch == 0 else []),
                            inc=(c == 3))
                    slot_free[sl_] = tmm4[3]
                if gu == 1:
                    for c in range(4):
                        tupst[c] = p.v("tensor_copy",
                                       bfs[:, c * 512:(c + 1) * 512],
                                       pp[c][:, 0:512],
                                       waits=[tmm4[3], t_up_stage], inc=True)
                        bank_free[c] = tupst[c]
            tlast_tr = None
            for c in range(4):
                i = g * 4 + c
                tgc = p.v("tensor_copy", fsc[:, :], pp[c][:, 0:512],
                          waits=[tmm4[3], t_fsc_f], inc=True)
                tg2 = p.v("tensor_tensor", fsc[:, :], fsc[:, :],
                          pp[4][:, 0:512], op=AL.mult,
                          waits=[tgc, t_rbg], inc=True)
                tsg = p.s("activation", fsc[:, :], fsc[:, :], AF.Silu,
                          waits=[tg2], inc=True)
                hsl = slice(i * 512, (i + 1) * 512)
                th = p.v("tensor_tensor", bufH[:, hsl], fsc[:, :],
                         bfs[:, c * 512:(c + 1) * 512], op=AL.mult,
                         waits=[tsg, tupst[c]], inc=True)
                t_h[i] = th
                t_fsc_f = th
                bank_free[c] = th
            t_up_stage = th
        bank_free[4] = tg2
        if stop_after == "l1f_gu" and half == 0:
            finish([t_h[63]] * NT)
            build_blocks(nc, p)
            es.close()
            return nc
        t_z = {}
        for m_ in range(32):
            ta = p.mm(pp[0][:, 0:512], H128T, bufH[:, (2 * m_) * 512:
                                                    (2 * m_ + 1) * 512],
                      True, True, waits=[t_h[2 * m_]] + bw(0), inc=True)
            tb = p.mm(pp[1][:, 0:512], H128T, bufH[:, (2 * m_ + 1) * 512:
                                                    (2 * m_ + 2) * 512],
                      True, True, waits=[t_h[2 * m_ + 1]] + bw(1), inc=True)
            tev0 = p.v("tensor_copy", fsc[:, :], pp[0][:, 0:512],
                       waits=[ta, t_fsc_f], inc=True)
            t_fsc_f = tev0
            tza = p.v("tensor_tensor", bufH[:, (2 * m_) * 512:(2 * m_ + 1) * 512],
                      fsc[:, :], pp[1][:, 0:512], op=AL.add,
                      waits=[tev0, tb], inc=True)
            tzb = p.v("tensor_tensor", bufH[:, (2 * m_ + 1) * 512:
                                            (2 * m_ + 2) * 512],
                      fsc[:, :], pp[1][:, 0:512], op=AL.subtract,
                      waits=[tev0, tb], inc=True)
            t_fsc_f = tzb
            t_z[2 * m_] = tza
            t_z[2 * m_ + 1] = tzb
            bank_free[0] = tzb
            bank_free[1] = tzb
        tprev = t_z[63]
        hh = 2
        while hh < 64:
            a_ap = (bufH[:, :].rearrange("p (g w t) -> p g w t",
                                         w=2 * hh, t=512)
                    .rearrange("p g w t -> p (g w t)") if False else None)
            a_ap = bufH[:, :].rearrange("p (g two w t) -> p g two w t",
                                        two=2, w=hh, t=512)
            tsum_ = p.v("tensor_tensor", a_ap[:, :, 0, :, :],
                        a_ap[:, :, 0, :, :], a_ap[:, :, 1, :, :],
                        op=AL.add, waits=[tprev], inc=True)
            tdiff = p.v("scalar_tensor_tensor", a_ap[:, :, 1, :, :],
                        a_ap[:, :, 1, :, :], -2.0, a_ap[:, :, 0, :, :],
                        op0=AL.mult, op1=AL.add, waits=[tsum_], inc=True)
            tprev = tdiff
            hh *= 2
        if stop_after == "l1f_fw" and half == 0:
            finish([tprev] * NT)
            build_blocks(nc, p)
            es.close()
            return nc
        tcm = p.v("tensor_reduce", fsc[:, :],
                  bufH[:, :].rearrange("p (ch t) -> p t ch", t=512),
                  axis=AX.X, op=AL.max, apply_absolute_value=True,
                  waits=[tprev, t_fsc_f], inc=True)
        tm2 = {}
        for sub in range(4):
            ttr = p.tr(pp[2][:, 0:128], fsc[:, sub * 128:(sub + 1) * 128],
                       IDENTF, waits=[tcm] + bw(2), inc=True)
            tm2[sub] = p.v("reduce_max", sml[:, 50:51], pp[2][:, 0:128],
                           axis=AX.X, waits=[ttr], inc=True)
            bank_free[2] = tm2[sub]
            j = half * 4 + sub
            trc_ = p.v("tensor_scalar", sml[:, 51:52], INVS(j), ISW(1),
                       FWHT_C, op0=AL.mult, op1=AL.mult, waits=[tm2[sub]],
                       inc=True)
            tmt = p.v("tensor_tensor", sml[:, 52:53], sml[:, 50:51],
                      sml[:, 51:52], op=AL.mult, waits=[trc_], inc=True)
            tc2 = p.v("tensor_scalar", sml[:, 53:54], sml[:, 52:53], 1e-5,
                      None, op0=AL.max, waits=[tmt], inc=True)
            td_ = p.v("reciprocal", sml[:, 54:55], sml[:, 53:54],
                      waits=[tc2], inc=True)
            tse = p.v("tensor_scalar", sml[:, 68 + sub:69 + sub],
                      sml[:, 54:55], sml[:, 51:52], 128.0, op0=AL.mult,
                      op1=AL.mult, waits=[td_], inc=True)
            tes = p.v("tensor_scalar", sml[:, 72 + sub:73 + sub],
                      sml[:, 53:54], ISW(2), 1.0 / 128.0, op0=AL.mult,
                      op1=AL.mult, waits=[tc2], inc=True)
        trow2 = []
        for sub in range(4):
            ttr = p.tr(pp[2][0:1, 0:128], sml[:, 68 + sub:69 + sub], IDENTF,
                       waits=[tse, tes] + bw(2), inc=True)
            tcp = p.v("tensor_copy", fsc[0:1, sub * 128:(sub + 1) * 128],
                      pp[2][0:1, 0:128], waits=[ttr, tcm], inc=True)
            bank_free[2] = tcp
            trow2.append(tcp)
        t_seb = p.mm(pp[4][:, 0:512], ONESROW, fsc[0:1, 0:512], True, True,
                     waits=trow2 + bw(4), inc=True)
        trow3 = []
        for sub in range(4):
            ttr = p.tr(pp[2][0:1, 0:128], sml[:, 72 + sub:73 + sub], IDENTF,
                       waits=[t_seb] + bw(2), inc=True)
            tcp = p.v("tensor_copy", fsc[0:1, sub * 128:(sub + 1) * 128],
                      pp[2][0:1, 0:128], waits=[ttr] + trow2, inc=True)
            bank_free[2] = tcp
            trow3.append(tcp)
        t_esb = p.mm(pp[5][:, 0:512], ONESROW, fsc[0:1, 0:512], True, True,
                     waits=trow3 + bw(5), inc=True)
        t_fsc_f = t_esb
        if stop_after == "l1f_se" and half == 0:
            finish([t_esb] * NT)
            build_blocks(nc, p)
            es.close()
            return nc
        t_q2 = {}
        for i in range(64):
            hsl = slice(i * 512, (i + 1) * 512)
            tu = p.v("tensor_tensor", fsc[:, :], bufH[:, hsl],
                     pp[4][:, 0:512], op=AL.mult,
                     waits=[t_seb, tprev, t_fsc_f], inc=True)
            tr1_ = p.v("tensor_scalar", fsc[:, :], fsc[:, :], MAGIC, None,
                       op0=AL.add, waits=[tu], inc=True)
            tr2_ = p.v("tensor_scalar", fsc[:, :], fsc[:, :], MAGIC, 127.0,
                       op0=AL.subtract, op1=AL.min, waits=[tr1_], inc=True)
            tr3_ = p.v("tensor_scalar", bufH[:, hsl], fsc[:, :], -128.0,
                       None, op0=AL.max, waits=[tr2_], inc=True)
            t_q2[i] = tr3_
            t_fsc_f = tr3_
        if stop_after == "l1f_q2" and half == 0:
            finish([t_q2[63]] * NT)
            build_blocks(nc, p)
            es.close()
            return nc
        for og2 in range(4):
            tmm4 = [None] * 4
            for kch in range(64):
                sl_ = kch % 4
                blk = (kch // 8) * 32 + og2 * 8 + (kch % 8)
                twl = p.dma(["sync", "scalar"][kch % 2],
                            bk[:, sl_ * 512:(sl_ + 1) * 512],
                            dt["ago_wd"][blk, :, :],
                            waits=([slot_free[sl_]] if slot_free[sl_]
                                   else [(SLOT_SEM[sl_], 0)]),
                            sem=SLOT_SEM[sl_])
                for c in range(4):
                    tmm4[c] = p.mm(
                        pp[c][:, 0:512],
                        bk[:, sl_ * 512 + c * 128:sl_ * 512 + c * 128 + 128],
                        bufH[:, kch * 512:(kch + 1) * 512],
                        start=(kch == 0), stop=(kch == 63),
                        waits=[twl, t_q2[kch]] + (bw(c) if kch == 0 else []),
                        inc=(c == 3))
                slot_free[sl_] = tmm4[3]
            for c in range(4):
                ot = og2 * 4 + c
                tyc = p.v("tensor_copy", fsc[:, :], pp[c][:, 0:512],
                          waits=[tmm4[3], t_fsc_f], inc=True)
                t_fsc_f = tyc
                ty = p.v("tensor_tensor", bfs[:, 0:512], fsc[:, :],
                         pp[5][:, 0:512], op=AL.mult,
                         waits=[tyc, t_esb, t_bfs_f], inc=True)
                bank_free[c] = ty
                for sub in range(4):
                    j = half * 4 + sub
                    bi = "b%d" % (sub % 2)
                    ttr = p.tr(ppb[sub % 2][:, 0:128],
                               bfs[:, sub * 128:(sub + 1) * 128], IDENT,
                               waits=[ty] + bw(bi), inc=True)
                    tres = p.v("tensor_tensor",
                               strm[:, j * D + ot * 128:j * D + ot * 128 + 128],
                               strm[:, j * D + ot * 128:j * D + ot * 128 + 128],
                               ppb[sub % 2][:, 0:128], op=AL.add,
                               waits=[ttr] + ([res4[(j, ot)]] if (j, ot) in res4
                                              else []),
                               inc=True)
                    res4[(j, ot)] = tres
                    bank_free[bi] = tres
                t_bfs_f = ttr
        t_bk_ffn = slot_free[0]
        t_bv_ffn = slot_free[1]
        bank_free[5] = ty
        t_A_f = ("tt", p.cnt["tt"])
        if stop_after == "l1f_dn" and half == 0:
            finish([res4[(3, 15)]] * NT)
            build_blocks(nc, p)
            es.close()
            return nc

    ln4 = layernorm_pass([res4[(j, 15)] for j in range(NT)], make_fm=False)
    finish(ln4)
    build_blocks(nc, p)
    es.close()
    return nc


def build_blocks(nc, p):
    with nc.Block() as block:
        @block.sync
        def _(eng):
            for op in p.ops["sync"]:
                op(eng)

        @block.scalar
        def _(eng):
            for op in p.ops["scalar"]:
                op(eng)

        @block.vector
        def _(eng):
            for op in p.ops["vector"]:
                op(eng)

        @block.tensor
        def _(eng):
            for op in p.ops["tensor"]:
                op(eng)

        @block.gpsimd
        def _(eng):
            for op in p.ops["gpsimd"]:
                op(eng)



def hadamard(n):
    h = np.ones((1, 1), np.float32)
    while h.shape[0] < n:
        h = np.block([[h, h], [h, -h]])
    return h.astype(np.float32)


def make_inputs(inputs):
    import ml_dtypes
    bf = ml_dtypes.bfloat16
    f32 = np.float32

    x = np.asarray(inputs["L0_attn_x"], f32)
    att0 = np.concatenate([np.ascontiguousarray(np.asarray(inputs[k], f32).T)
                           for k in ("L0_attn_wq", "L0_attn_wk",
                                     "L0_attn_wv", "L0_attn_wo")], axis=0)
    gu_t = np.ascontiguousarray(np.asarray(inputs["L0_ffn_w_gate_up"], f32).T)
    proj_t = np.ascontiguousarray(np.asarray(inputs["L0_ffn_w_proj"], f32).T)
    att1 = np.concatenate([np.ascontiguousarray(np.asarray(inputs[k], f32).T)
                           for k in ("L1_attn_Wq", "L1_attn_Wk",
                                     "L1_attn_Wv", "L1_attn_Wo")], axis=0)
    perm = (np.arange(FF_H) % 128) * 64 + np.arange(FF_H) // 128
    wg_t = np.ascontiguousarray(np.asarray(inputs["L1_ffn_w_gate"], f32).T[:, perm])
    wu_t = np.ascontiguousarray(np.asarray(inputs["L1_ffn_w_up"], f32).T[:, perm])
    wd_t = np.ascontiguousarray(np.asarray(inputs["L1_ffn_w_down"], f32).T[perm, :])

    fbias = np.stack([np.asarray(inputs["L0_attn_fq1_b"], f32),
                      np.asarray(inputs["L0_attn_fq2_b"], f32),
                      np.asarray(inputs["L0_attn_fk1_b"], f32),
                      np.asarray(inputs["L0_attn_fk2_b"], f32)], axis=1)
    normw = np.asarray(inputs["L0_attn_norm_w"], f32).reshape(128, 1)
    l1bias_pp = np.zeros((128, 64), f32)
    for m, k in enumerate(("L1_attn_bq", "L1_attn_bk", "L1_attn_bv",
                           "L1_attn_bo")):
        l1bias_pp[:, m * 16:(m + 1) * 16] = np.asarray(
            inputs[k], f32).reshape(16, 128).T
    borow = np.asarray(inputs["L1_attn_bo"], f32).reshape(1, D)

    h128 = hadamard(128).astype(bf)
    b2 = np.zeros((2, 128), f32)
    b2[0, :64] = 1.0
    b2[1, 64:] = 1.0

    half = DK // 2
    inv_freq = (1.0 / 10000.0 ** (np.arange(half, dtype=f32)
                                  / f32(half))).astype(f32)

    cval_arr = np.tile(np.array(CONST_VALS, f32)[None, :], (128, 1))
    in_maps = []
    for c in range(NC_):
        seq, r = c // GRP, c % GRP
        tiles = [r + 4 * i for i in range(NT)]
        rows = np.concatenate([np.arange(g * 128, (g + 1) * 128)
                               for g in tiles])
        xs = np.ascontiguousarray(x[seq].reshape(SEQ, D)[rows])
        positions = rows.astype(f32)
        th = positions[:, None] * inv_freq[None, :]
        cos = np.repeat(np.cos(th), 2, axis=-1).astype(f32)
        sin = np.repeat(np.sin(th), 2, axis=-1).astype(f32)
        cos_t = np.ascontiguousarray(np.tile(cos, (1, NH)))
        sin_t = np.ascontiguousarray(np.tile(sin, (1, NH)))
        qr = np.arange(128)[:, None]
        kc = np.arange(512)[None, :]
        diagmask = (kc <= r * 128 + qr).astype(f32).astype(bf)
        partflag = np.zeros((128, 3), f32)
        for rp in range(3):
            partflag[:, rp] = 1.0 if rp < r else 0.0

        m = {
            "x": xs,
            "att0_s": np.ascontiguousarray(
                att0.reshape(4, D, D)[:, c * 256:(c + 1) * 256, :]
                .reshape(1024, D)),
            "fq1t": np.ascontiguousarray(np.asarray(inputs["L0_attn_fq1_w"], f32).T),
            "fq2t": np.ascontiguousarray(np.asarray(inputs["L0_attn_fq2_w"], f32).T),
            "fk1t": np.ascontiguousarray(np.asarray(inputs["L0_attn_fk1_w"], f32).T),
            "fk2t": np.ascontiguousarray(np.asarray(inputs["L0_attn_fk2_w"], f32).T),
            "fbias": fbias,
            "normw": normw,
            "gu_s": np.ascontiguousarray(gu_t[c * 256:(c + 1) * 256, :]),
            "proj_s": np.ascontiguousarray(proj_t[c * 1024:(c + 1) * 1024, :]),
            "att1_s": np.ascontiguousarray(
                att1.reshape(4, D, D)[:, c * 256:(c + 1) * 256, :]
                .reshape(1024, D)),
            "l1bias_pp": l1bias_pp,
            "borow": borow,
            "wg_s": np.ascontiguousarray(wg_t[c * 256:(c + 1) * 256, :]),
            "wu_s": np.ascontiguousarray(wu_t[c * 256:(c + 1) * 256, :]),
            "wd_s": np.ascontiguousarray(wd_t[c * 1024:(c + 1) * 1024, :]),
            "cos_t": cos_t.astype(bf),
            "sin_t": sin_t.astype(bf),
            "ident": np.eye(128, dtype=f32).astype(bf),
            "identf": np.eye(128, dtype=f32),
            "h128c": h128,
            "b2c": b2.astype(bf),
            "b2tc": np.ascontiguousarray(b2.T).astype(bf),
            "ones64": np.ones((64, 1), f32).astype(bf),
            "ones128": np.ones((128, 1), f32).astype(bf),
            "onesrow_f32": np.ones((1, 128), f32),
            "tri64": (np.arange(64)[:, None] <= np.arange(64)[None, :])
                     .astype(f32).astype(bf),
            "diagmask": diagmask,
            "partflag": partflag,
            "cvals": cval_arr,
            "cvalsb": cval_arr.astype(bf),
        }
        in_maps.append(m)
    return in_maps


def assemble(results):
    f32 = np.float32
    out = np.empty((B_, SEQ, D), f32)
    for c in range(NC_):
        seq, r = c // GRP, c % GRP
        o = np.asarray(results[c]["out"], f32)
        for i in range(NT):
            g = r + 4 * i
            out[seq, g * 128:(g + 1) * 128, :] = o[i * 128:(i + 1) * 128, :]
    return out


def kernel(**inputs):
    in_maps = make_inputs(inputs)
    nc = build_program()
    res = run_bass_kernel_spmd(nc, in_maps, core_ids=list(range(NC_)))
    return assemble(res.results)
